# version 06
# baseline (speedup 1.0000x reference)
import numpy as np
import concourse.bass as bass
import concourse.mybir as mybir
from concourse.bass_utils import run_bass_kernel_spmd

AF = mybir.ActivationFunctionType
ALU = mybir.AluOpType
AX = mybir.AxisListType
F32 = mybir.dt.float32
BF16 = mybir.dt.bfloat16


COMPUTE_ENGINES = ("tensor", "vector", "scalar", "gpsimd")
DMA_QUEUES = {"sync": "sync", "gpsimd_dma": "gpsimd", "scalar_dma": "scalar"}
ALL_STREAMS = tuple(COMPUTE_ENGINES) + tuple(DMA_QUEUES)


class Buf:
    __slots__ = ("name", "writer", "readers")

    def __init__(self, name):
        self.name = name
        self.writer = None
        self.readers = []


class DmaSlot:
    __slots__ = ("name", "handle", "count", "is_output")

    def __init__(self, name, is_output=False):
        self.name = name
        self.handle = None
        self.count = 0
        self.is_output = is_output


class Op:
    __slots__ = ("engine", "fn", "deps", "need_inc", "sem", "val", "g",
                 "scope")

    def __init__(self, engine, fn, g, scope):
        self.engine = engine
        self.fn = fn
        self.deps = []
        self.need_inc = False
        self.sem = None
        self.val = 0
        self.g = g
        self.scope = scope


class Em:
    def __init__(self, nc, chain=False):
        self.nc = nc
        self.chain = chain
        self.ops = {e: [] for e in ALL_STREAMS}
        self._chain_buf = Buf("_chain")
        self._g = 0
        self._slots = []
        self.scope = "init"

    def op(self, engine, fn, reads=(), writes=(), sreads=()):
        o = Op(engine, fn, self._g, self.scope)
        self._g += 1
        self._track(o, list(reads), list(writes), list(sreads))
        self.ops[engine].append(o)
        return o

    def dma(self, queue, fn, slot, reads=(), writes=()):
        o = Op(queue, fn, self._g, self.scope)
        self._g += 1
        o.sem = slot
        self._track(o, list(reads), list(writes), [])
        self.ops[queue].append(o)
        return o

    def make_slot(self, name, is_output=False):
        s = DmaSlot(name, is_output)
        self._slots.append(s)
        return s

    def _track(self, o, reads, writes, sreads):
        if self.chain:
            reads = reads + [self._chain_buf]
            writes = writes + [self._chain_buf]
        deps = []
        for b in reads:
            if b.writer is not None:
                deps.append((b.writer, False))
        for b in sreads:
            if b.writer is not None:
                deps.append((b.writer, True))
        for b in writes:
            if b.writer is not None:
                deps.append((b.writer, False))
            deps.extend((r, False) for r in b.readers)
        strictmap = {}
        for d, strict in deps:
            if d is o:
                continue
            strictmap[id(d)] = strictmap.get(id(d), False) or strict
        added = set()
        for d, _ in deps:
            if d is o or id(d) in added:
                continue
            added.add(id(d))
            strict = strictmap[id(d)]
            if not strict and d.engine == o.engine == "tensor":
                continue
            o.deps.append(d)
            d.need_inc = True
        for b in reads:
            b.readers.append(o)
        for b in sreads:
            b.readers.append(o)
        for b in writes:
            b.writer = o
            b.readers = []

    def finish(self):
        nc = self.nc
        clock = {e: nc.alloc_semaphore(f"clk_{e}") for e in COMPUTE_ENGINES}
        for s in self._slots:
            s.handle = nc.alloc_semaphore(f"dma_{s.name}")
        for e in COMPUTE_ENGINES:
            v = 0
            for o in self.ops[e]:
                if o.need_inc:
                    v += 1
                o.val = v
                o.sem = clock[e]
        for q in DMA_QUEUES:
            for o in self.ops[q]:
                o.sem.count += 16
                o.val = o.sem.count
        streams = {}
        for name, lst in self.ops.items():
            phys = DMA_QUEUES.get(name, name)
            streams.setdefault(phys, []).extend(lst)
        for phys in streams:
            streams[phys].sort(key=lambda o: o.g)
        out_slots = [s for s in self._slots if s.is_output and s.count > 0]
        watermark = {}

        def emit_stream(phys, eng, lst, final):
            cur_scope = None
            cur_sid = None
            for o in lst:
                if o.scope != cur_scope:
                    if cur_scope is not None:
                        nc.leave_named_scope(f"ph_{cur_scope}", cur_sid, False)
                    cur_sid, _ = nc.enter_named_scope(f"ph_{o.scope}", False)
                    cur_scope = o.scope
                for d in o.deps:
                    sem = d.sem.handle if isinstance(d.sem, DmaSlot) else d.sem
                    key = (phys, id(sem))
                    if watermark.get(key, 0) >= d.val:
                        continue
                    eng.wait_ge(sem, d.val)
                    watermark[key] = d.val
                inst = o.fn(eng)
                if isinstance(o.sem, DmaSlot):
                    inst.then_inc(o.sem.handle, 16)
                elif o.need_inc:
                    inst.then_inc(o.sem, 1)
            if cur_scope is not None:
                nc.leave_named_scope(f"ph_{cur_scope}", cur_sid, False)
            if final:
                for s in out_slots:
                    key = (phys, id(s.handle))
                    if watermark.get(key, 0) < s.count:
                        eng.wait_ge(s.handle, s.count)
                        watermark[key] = s.count

        with nc.Block("main") as block:
            for phys in ["sync", "tensor", "vector", "scalar", "gpsimd"]:
                lst = streams.get(phys, [])

                def body(eng, lst=lst, phys=phys):
                    emit_stream(phys, eng, lst, final=(phys == "sync"))

                getattr(block, phys)(body)


D = 1024
T = 2048
NSEQ = 4
NCORE = 8
NH = 16
HME = 1408
NE = 8
DFF = 4096
NC_D = D // 128
NC_H = HME // 128
NC_F = DFF // 128
NT = T // 128
NTT = T // 512


def build_nc(cfg=None):
    cfg = cfg or {}
    nseq = cfg.get("nseq", NSEQ)
    upto = cfg.get("upto", "comb")
    dbg = set(cfg.get("debug", ()))

    PHASES = ["ln", "gate", "moe", "qkv", "attn", "proj", "mlp", "comb"]
    run_ph = [] if upto == "none" else PHASES[:PHASES.index(upto) + 1]

    nc = bass.Bass("TRN2", target_bir_lowering=False, debug=False)
    em = Em(nc, chain=cfg.get("chain", False))
    skip = set(cfg.get("skip", ()))

    dr = {}

    def din(name, shape):
        dr[name] = nc.dram_tensor(name, shape, F32, kind="ExternalInput").ap()

    din("x", [nseq, T, D])
    din("w_qk", [16, 128, 8, 128])
    din("b_qk", [128, 16])
    din("w_v", [8, 128, 1024])
    din("b_v", [1, 1024])
    din("w_proj", [8, 128, 8, 128])
    din("b_proj", [128, 8])
    din("w_fc", [32, 128, 8, 128])
    din("b_fc", [128, 32])
    din("w_mlp", [8, 2, 128, 16, 128])
    din("b_mlp", [128, 8])
    din("gate_w", [128, 8, 8])
    din("gate_b", [1, 8])
    din("w1", [NE, NC_H, 128, 8, 128])
    din("w3", [NE, NC_H, 128, 8, 128])
    din("w2", [NE, NC_H, 128, 1024])
    din("b1", [NE, 128, NC_H])
    din("b3", [NE, 128, NC_H])
    din("b2", [NE, 1024])
    din("sw1", [NC_H, 128, 8, 128])
    din("sw3", [NC_H, 128, 8, 128])
    din("sw2", [NC_H, 128, 1024])
    din("sb1", [128, NC_H])
    din("sb3", [128, NC_H])
    din("sb2", [1, 1024])
    x_d = dr["x"]

    out_d = nc.dram_tensor("out", [nseq, T, D], F32, kind="ExternalOutput").ap()
    g_spill = nc.dram_tensor("g_spill", [nseq, T, D], BF16, kind="Internal").ap()
    wc = {}
    for nm, shape in [("w13", [9, NC_H, 128, 2048]), ("w2c", [9, NC_H, 128, 1024]),
                      ("qk", [16, 128, 1024]), ("wv", [8, 128, 1024]),
                      ("wp", [8, 128, 1024]), ("wf", [32, 128, 1024]),
                      ("wm", [8, 2, 128, 2048])]:
        wc[nm] = nc.dram_tensor(f"wc_{nm}", shape, BF16, kind="Internal").ap()
    ln1_spill = nc.dram_tensor("ln1_spill", [nseq, NT, 128, D], BF16,
                               kind="Internal").ap()
    dbg_d = {}
    for name, shape, dt_ in [
            ("dbg_xnT", [nseq, 128, NC_D * T], BF16),
            ("dbg_rw", [nseq, 128, NT * 8], F32),
            ("dbg_g", [nseq, T, D], BF16),
            ("dbg_q", [nseq, 128, NC_D * T], BF16),
            ("dbg_k", [nseq, 128, NC_D * T], BF16),
            ("dbg_v", [nseq, 128, NT * 1040], BF16),
            ("dbg_y", [nseq, 128, NC_D * T], BF16),
            ("dbg_t", [nseq, 128, NC_D * T], BF16),
            ("dbg_s", [NSEQ, 128, NC_D * T], BF16)]:
        if name in dbg:
            dbg_d[name] = nc.dram_tensor(name, shape, dt_,
                                         kind="ExternalOutput").ap()

    sbt = {}

    def asb(name, shape, dtype):
        sbt[name] = nc.alloc_sbuf_tensor(name, shape, dtype).ap()
        return sbt[name]

    xn_T = asb("xn_T", [128, NC_D * T], BF16)
    A2 = asb("A2", [128, 16384], F32)
    A3 = asb("A3", [128, 16640], BF16)
    A4 = asb("A4", [128, 8704], F32)
    A5 = asb("A5", [128, 16384], BF16)

    A2bf = A2.bitcast(BF16)
    y_acc = A2
    q_T = A2bf[:, 0:NC_D * T]
    k_T = A2bf[:, NC_D * T:2 * NC_D * T]
    h_mlp = A2bf

    h_moe = A3[:, 0:6 * T]
    v_aug = A3[:, 0:NT * 1040]
    ln2_bf = A3[:, 0:NC_D * 1024]

    y_T = A5
    s_T = A5
    w2_bf = A5[:, 0:NC_H * 1024]
    ln1_full = A5

    A4bf = A4.bitcast(BF16)

    def a4f(c0, n):
        return A4[:, c0:c0 + n]

    def a4b(c0f, n):
        return A4bf[:, 2 * c0f:2 * c0f + n]

    kst = a4f(0, 1024)
    xst = [a4f(0, 1024), a4f(1024, 1024)]
    xn_f = a4f(2048, 1024)
    xn_b = a4b(3072, 1024)
    ln1_b = a4b(3584, 1024)
    st6s = [a4f(4096, 16), a4f(4608, 16)]
    gsc = a4f(4200, 32)
    rwT_bf = a4b(4300, 2048)
    w13st = a4f(0, 2048)
    w13bf = [a4b(2048, 2048), a4b(3072, 2048)]
    w2st = a4f(5376, 1024)
    u1s = [a4f(6400, 512), a4f(6912, 512)]
    gst = [a4b(7424, 1024), a4b(7936, 1024)]
    wqkst = a4f(0, 1024)
    wqkbf = [a4b(1024, 1024), a4b(1536, 1024)]
    wvst = a4f(2048, 1024)
    wv_bf = a4b(3072, 8192)
    bnd = a4f(7168, 8)
    expT = a4b(0, 4096)
    recipb = a4f(4096, 512)
    srow = a4f(4608, 512)
    wpst = a4f(5248, 1024)
    wpbf = [a4b(6272, 1024), a4b(6784, 1024)]
    wfst = a4f(0, 1024)
    wfbf = [a4b(1024, 1024), a4b(1536, 1024)]
    wmst = a4f(2048, 2048)
    wmbf = [a4b(4096, 2048), a4b(5120, 2048)]
    ab2s = a4f(6144, 1024)
    sqrow = a4f(6144, 512)
    sqb = a4b(7168, 1024)
    mrow = a4f(7680, 512)
    vrow = a4f(8192, 512)
    cxst = [a4f(0, 1024), a4f(1024, 1024)]
    cgst = [a4b(2048, 1024), a4b(2560, 1024)]
    sgf = a4f(3072, 1024)
    ost = [a4f(4096, 1024), a4f(5120, 1024)]

    ones128 = asb("ones128", [128, 1], BF16)
    ones1 = asb("ones1", [1, 128], BF16)
    ones1f = asb("ones1f", [1, 128], F32)
    id_bf = asb("id_bf", [128, 128], BF16)
    id_f32 = asb("id_f32", [128, 128], F32)
    mask_m = asb("mask_m", [128, 1024], BF16)
    cpool = asb("cpool", [128, 8], F32)
    CONSTS = {0.0: 0, 1e-5: 1, 1e-10: 2, 1.0: 3}

    def ccol(v):
        i = CONSTS[v]
        return cpool[:, i:i + 1]

    bqk_c = asb("bqk_c", [128, 16], F32)
    bproj_c = asb("bproj_c", [128, 8], F32)
    bfc_c = asb("bfc_c", [128, 32], F32)
    bmlp_c = asb("bmlp_c", [128, 8], F32)
    b1_c = asb("b1_c", [128, NC_H * 9], F32)
    b3_c = asb("b3_c", [128, NC_H * 9], F32)
    gw_bf = asb("gw_bf", [128, 64], BF16)
    gb_b = asb("gb_b", [128, 8], F32)
    gbst = asb("gbst", [1, 8], F32)
    b2all = asb("b2all", [8, 1024], BF16)
    bv_bf = asb("bv_bf", [1, 1024], BF16)
    sb2_bf = asb("sb2_bf", [1, 1024], BF16)
    negb = asb("negb", [128, 1], F32)
    rw_tok = asb("rw_tok", [128, NT * 8], F32)

    pst = {}

    def aps(name, shape):
        pst[name] = nc.alloc_psum_tensor(name, shape, F32).ap()
        return pst[name]

    PA = aps("PA", [128, 512])
    PB = aps("PB", [128, 512])
    PC = aps("PC", [128, 512])
    PD = aps("PD", [128, 512])
    PE2 = aps("PE2", [128, 1024])
    PF2 = aps("PF2", [128, 1024])
    PA_bf = PA.bitcast(BF16)
    PB_bf = PB.bitcast(BF16)
    PC_bf = PC.bitcast(BF16)
    PD_bf = PD.bitcast(BF16)

    bufs = {}

    def B(name):
        if name not in bufs:
            bufs[name] = Buf(name)
        return bufs[name]

    def _bl(arena, c0, n, blk=512):
        k0 = c0 // blk
        k1 = (c0 + n - 1) // blk
        return [B(f"{arena}_{k}") for k in range(k0, k1 + 1)]

    def A4BL(c0, n):
        return _bl("A4", c0, n)

    def A4BLb(c0b, nb):
        return _bl("A4", c0b // 2, (nb + 1) // 2)

    def A2BL(c0, n):
        return _bl("A2", c0, n)

    def A2BLb(c0b, nb):
        return _bl("A2", c0b // 2, (nb + 1) // 2)

    def A3BL(c0b, nb):
        return _bl("A3", c0b, nb)

    def A5BL(c0b, nb):
        return _bl("A5", c0b, nb)

    QT_B = A2BLb(0, 16384)
    KT_B = A2BLb(16384, 16384)
    YACC_B = A2BL(0, 16384)
    HMLP_B = A2BLb(0, 32768)
    HMOE_B = A3BL(0, 22528)
    VAUG_B = A3BL(0, 16640)
    LN2_B = A3BL(0, 8192)
    YT_B = A5BL(0, 16384)
    W2BF_B = A5BL(0, 11264)
    WVBF_B = A5BL(0, 8192)
    LN1F_B = A4BL(0, 8192)
    EXPT_B = A4BL(0, 4096)

    SL = {}

    def slot(name, is_output=False):
        if name not in SL:
            SL[name] = em.make_slot(name, is_output)
        return SL[name]

    V, S, Tn, G = "vector", "scalar", "tensor", "gpsimd"

    def ld(dst, src, slotname, writebuf, reads=()):
        wl = writebuf if isinstance(writebuf, list) else [B(writebuf)]
        em.dma("sync", lambda e, d=dst, s=src: e.dma_start(out=d, in_=s),
               slot(slotname), reads=list(reads), writes=wl)

    def stdma(dst, src, slotname, readbufs, is_output=False, writebuf=None,
              readbl=()):
        em.dma("gpsimd_dma",
               lambda e, d=dst, s=src: e.dma_start(out=d, in_=s),
               slot(slotname, is_output),
               reads=[B(x) for x in readbufs] + list(readbl),
               writes=[B(writebuf)] if writebuf else [])

    def wstream(s_, cls, key, stage_ap, stage_bl, stage_slot, f32_src,
                bf_ap, bf_bl, cache_ap, ceng="gpsimd", sfx=""):
        ebuf = B(f"wc_{cls}_{key}")
        if isinstance(s_, tuple):
            s_ = 0 if all(v == 0 for v in s_) else 1
        if s_ == 0:
            ld(stage_ap, f32_src, stage_slot, stage_bl)
            em.op(ceng, lambda e: e.tensor_copy(bf_ap, stage_ap),
                  reads=stage_bl + bf_bl, writes=bf_bl)
            em.dma("gpsimd_dma",
                   lambda e: e.dma_start(out=cache_ap, in_=bf_ap),
                   slot(f"wcs_{cls}"), reads=bf_bl,
                   writes=[ebuf, B(f"wcc_{cls}")])
        else:
            em.dma("sync", lambda e: e.dma_start(out=bf_ap, in_=cache_ap),
                   slot(f"wcl_{cls}{sfx}"), reads=[ebuf] + bf_bl,
                   writes=bf_bl)

    if not cfg.get("no_consts"):
        for v, i in CONSTS.items():
            em.op(G, lambda e, v=v, i=i: e.memset(cpool[:, i:i + 1], v),
                  writes=[B("cpool")])
        for t_, bn_, val in ((ones128, "ones128", 1.0), (ones1, "ones1", 1.0),
                             (ones1f, "ones1f", 1.0)):
            em.op(G, lambda e, t_=t_, val=val: e.memset(t_, val),
                  writes=[B(bn_)])
        for t_, bn_ in ((id_bf, "id_bf"), (id_f32, "id_f32")):
            em.op(G, lambda e, t_=t_: e.memset(t_, 0.0), writes=[B(bn_)])
            em.op(G, lambda e, t_=t_: e.affine_select(
                out=t_, in_=t_, compare_op=ALU.not_equal, fill=1.0, base=0,
                pattern=[[-1, 128]], channel_multiplier=1),
                reads=[B(bn_)], writes=[B(bn_)])
        em.op(G, lambda e: e.memset(mask_m, 1.0), writes=[B("mask_m")])
        em.op(G, lambda e: e.affine_select(
            out=mask_m, in_=mask_m, compare_op=ALU.is_ge, fill=0.0, base=-512,
            pattern=[[1, 1024]], channel_multiplier=-1),
            reads=[B("mask_m")], writes=[B("mask_m")])

        def kld(dst, src, bl):
            em.dma("sync", lambda e, d=dst, s=src: e.dma_start(out=d, in_=s),
                   slot("kconst"), reads=[], writes=bl + [B("kchain")])

        kld(bqk_c, dr["b_qk"], [B("bqk_c")])
        kld(bproj_c, dr["b_proj"], [B("bproj_c")])
        kld(bfc_c, dr["b_fc"], [B("bfc_c")])
        kld(bmlp_c, dr["b_mlp"], [B("bmlp_c")])
        for e_ in range(NE):
            kld(b1_c[:, e_ * NC_H:(e_ + 1) * NC_H], dr["b1"][e_], [B("b1_c")])
            kld(b3_c[:, e_ * NC_H:(e_ + 1) * NC_H], dr["b3"][e_], [B("b3_c")])
        kld(b1_c[:, 8 * NC_H:9 * NC_H], dr["sb1"], [B("b1_c")])
        kld(b3_c[:, 8 * NC_H:9 * NC_H], dr["sb3"], [B("b3_c")])
        KST_B = A4BL(0, 1024)
        kld(kst[:, 0:64], dr["gate_w"].rearrange("p a b -> p (a b)"), KST_B)
        em.op(V, lambda e: e.tensor_copy(gw_bf, kst[:, 0:64]),
              reads=KST_B + [B("gw_bf")], writes=[B("gw_bf")])
        kld(kst[0:8, :], dr["b2"], KST_B)
        em.op(V, lambda e: e.tensor_copy(b2all, kst[0:8, :]),
              reads=KST_B + [B("b2all")], writes=[B("b2all")])
        kld(kst[0:1, :], dr["b_v"], KST_B)
        em.op(V, lambda e: e.tensor_copy(bv_bf, kst[0:1, :]),
              reads=KST_B + [B("bv_bf")], writes=[B("bv_bf")])
        kld(kst[0:1, :], dr["sb2"], KST_B)
        em.op(V, lambda e: e.tensor_copy(sb2_bf, kst[0:1, :]),
              reads=KST_B + [B("sb2_bf")], writes=[B("sb2_bf")])
        kld(gbst, dr["gate_b"], [B("gbst")])
        em.op(Tn, lambda e: e.matmul(PA[:, 0:8], ones1f, gbst,
                                     start=True, stop=True),
              reads=[B("ones1f"), B("gbst")], writes=[B("PA")])
        em.op(V, lambda e: e.tensor_copy(gb_b, PA[:, 0:8]),
              reads=[B("PA")], writes=[B("gb_b")])

    def xv(c, lo, n):
        return xn_T[:, c * T + lo: c * T + lo + n]

    XST_B = [A4BL(0, 1024), A4BL(1024, 1024)]
    XNF_B = A4BL(2048, 1024)
    XNB_B = A4BL(3072, 512)
    LN1B_B = A4BL(3584, 512)
    ST6_BS = [A4BL(4096, 16), A4BL(4608, 16)]

    def seq_ln(s):
        for i in range(NT):
            par = i % 2
            xs = xst[par]
            st6 = st6s[par]
            ST6_B = ST6_BS[par]
            ld(xs, x_d[s, i * 128:(i + 1) * 128, :], f"xld{par}", XST_B[par])
            for (src, srcb, eps, dstb, dst) in (
                    (xs, XST_B[par], 1e-5, XNF_B, xn_f),
                    (xn_f, XNF_B, 1e-10, LN1B_B, ln1_b)):
                em.op(V, lambda e, src=src, st6=st6: e.bn_stats(
                        st6[:, 0:6], src[:, 0:512]),
                      reads=srcb + ST6_B, writes=ST6_B)
                em.op(V, lambda e, src=src, st6=st6: e.bn_stats(
                        st6[:, 6:12], src[:, 512:1024]),
                      reads=srcb + ST6_B, writes=ST6_B)
                em.op(V, lambda e, st6=st6: e.bn_aggr(st6[:, 12:14],
                                                      st6[:, 0:12]),
                      reads=ST6_B, writes=ST6_B)
                em.op(S, lambda e, eps=eps, st6=st6: e.activation(
                        st6[:, 14:15], st6[:, 13:14], AF.Sqrt,
                        bias=ccol(eps), scale=1.0),
                      reads=ST6_B, sreads=[B("cpool")], writes=ST6_B)
                em.op(V, lambda e, st6=st6: e.reciprocal(st6[:, 14:15],
                                                         st6[:, 14:15]),
                      reads=ST6_B, writes=ST6_B)
                em.op(V, lambda e, st6=st6: e.tensor_scalar(
                        st6[:, 15:16], st6[:, 12:13], st6[:, 14:15], -1.0,
                        ALU.mult, ALU.mult),
                      reads=ST6_B, sreads=ST6_B, writes=ST6_B)
                em.op(V, lambda e, src=src, dst=dst, st6=st6: e.tensor_scalar(
                        dst, src, st6[:, 14:15], st6[:, 15:16],
                        ALU.mult, ALU.add),
                      reads=srcb, sreads=ST6_B, writes=dstb)
            em.op(V, lambda e: e.tensor_copy(xn_b, xn_f),
                  reads=XNF_B + XNB_B, writes=XNB_B)
            for half in range(2):
                px = (PA_bf, PB_bf)[half]
                pxb = ("PA", "PB")[half]
                pl = (PC_bf, PD_bf)[half]
                plb = ("PC", "PD")[half]
                for c4 in range(4):
                    c = half * 4 + c4
                    em.op(Tn, lambda e, px=px, c4=c4, c=c: e.transpose(
                            px[:, c4 * 128:(c4 + 1) * 128],
                            xn_b[:, c * 128:(c + 1) * 128], id_bf),
                          reads=XNB_B + [B("id_bf")], writes=[B(pxb)])
                    em.op(Tn, lambda e, pl=pl, c4=c4, c=c: e.transpose(
                            pl[:, c4 * 128:(c4 + 1) * 128],
                            ln1_b[:, c * 128:(c + 1) * 128], id_bf),
                          reads=LN1B_B + [B("id_bf")], writes=[B(plb)])
                em.op(V, lambda e, px=px, half=half, i=i: e.tensor_copy(
                        xn_T.rearrange("p (c t) -> p c t", t=T)
                        [:, half * 4:(half + 1) * 4, i * 128:(i + 1) * 128],
                        px[:, 0:512].rearrange("p (c t) -> p c t", t=128)),
                      reads=[B(pxb), B("xn_T")], writes=[B("xn_T")])
                em.op(S, lambda e, pl=pl, half=half: e.copy(
                        ln1_b[:, half * 512:(half + 1) * 512], pl[:, 0:512]),
                      reads=[B(plb)] + LN1B_B, writes=LN1B_B)
            if "spill" not in skip:
                stdma(ln1_spill[s, i], ln1_b, f"ln1sp{par}",
                      [], is_output=False, writebuf=f"ln1d{par}",
                      readbl=LN1B_B)
        if "dbg_xnT" in dbg_d:
            stdma(dbg_d["dbg_xnT"][s], xn_T, "dbgx", ["xn_T"])

    GSC_B = A4BL(4200, 32)
    RWT_B = A4BL(4300, 1024)

    def seq_gate(s):
        for i in range(NT):
            pg = PA[:, 0:8]
            for dc in range(NC_D):
                em.op(Tn, lambda e, dc=dc, i=i: e.matmul(
                        pg, xv(dc, i * 128, 128),
                        gw_bf[:, dc * 8:(dc + 1) * 8],
                        start=(dc == 0), stop=(dc == NC_D - 1)),
                      reads=[B("xn_T"), B("gw_bf")], writes=[B("PA")])
            em.op(V, lambda e: e.reduce_max(gsc[:, 8:9], pg, axis=AX.X,
                                            negate=True),
                  reads=[B("PA")] + GSC_B, writes=GSC_B)
            em.op(S, lambda e: e.activation(gsc[:, 0:8], pg, AF.Exp,
                                            bias=gsc[:, 8:9], scale=1.0,
                                            accum_out=gsc[:, 9:10]),
                  reads=[B("PA")] + GSC_B, sreads=GSC_B, writes=GSC_B)
            em.op(V, lambda e: e.reciprocal(gsc[:, 9:10], gsc[:, 9:10]),
                  reads=GSC_B, writes=GSC_B)
            em.op(V, lambda e: e.tensor_scalar(gsc[:, 0:8], gsc[:, 0:8],
                                               gsc[:, 9:10], None, ALU.mult),
                  reads=GSC_B, sreads=GSC_B, writes=GSC_B)
            em.op(V, lambda e: e.tensor_tensor(gsc[:, 16:24], gsc[:, 0:8],
                                               gb_b, ALU.add),
                  reads=GSC_B + [B("gb_b")], writes=GSC_B)
            em.op(V, lambda e: e.reduce_max(gsc[:, 10:11], gsc[:, 16:24],
                                            axis=AX.X),
                  reads=GSC_B, writes=GSC_B)
            em.op(V, lambda e: e.tensor_scalar(gsc[:, 24:32], gsc[:, 16:24],
                                               gsc[:, 10:11], None, ALU.is_ge),
                  reads=GSC_B, sreads=GSC_B, writes=GSC_B)
            em.op(V, lambda e: e.scalar_tensor_tensor(
                    gsc[:, 24:32], gsc[:, 24:32], -1e30, gsc[:, 16:24],
                    ALU.mult, ALU.add),
                  reads=GSC_B, writes=GSC_B)
            em.op(V, lambda e: e.reduce_max(gsc[:, 11:12], gsc[:, 24:32],
                                            axis=AX.X),
                  reads=GSC_B, writes=GSC_B)
            em.op(V, lambda e: e.tensor_scalar(gsc[:, 24:32], gsc[:, 16:24],
                                               gsc[:, 11:12], None, ALU.is_ge),
                  reads=GSC_B, sreads=GSC_B, writes=GSC_B)
            em.op(V, lambda e, i=i: e.tensor_tensor(
                    rw_tok[:, i * 8:(i + 1) * 8], gsc[:, 0:8],
                    gsc[:, 24:32], ALU.mult),
                  reads=GSC_B + [B("rw_tok")], writes=[B("rw_tok")])
            em.op(Tn, lambda e, i=i: e.transpose(
                    PB[0:8, 0:128], rw_tok[:, i * 8:(i + 1) * 8], id_f32),
                  reads=[B("rw_tok"), B("id_f32")], writes=[B("PB")])
            em.op(V, lambda e, i=i: e.tensor_copy(
                    rwT_bf[0:8, i * 128:(i + 1) * 128], PB[0:8, 0:128]),
                  reads=[B("PB")] + RWT_B, writes=RWT_B)
        if "dbg_rw" in dbg_d:
            stdma(dbg_d["dbg_rw"][s], rw_tok, "dbgrw", ["rw_tok"])

    W13ST_B = [A4BL(0, 1024), A4BL(1024, 1024)]
    W13BF_B = [A4BL(2048, 1024), A4BL(3072, 1024)]
    W2ST_B = A4BL(5376, 1024)
    U1S_B = [A4BL(6400, 512), A4BL(6912, 512)]
    GST_B = [A4BL(7424, 512), A4BL(7936, 512)]

    def seq_moe(s):
        for e_ in [8] + list(range(NE)):
            shared = (e_ == 8)
            w1d = dr["sw1"] if shared else dr["w1"][e_]
            w3d = dr["sw3"] if shared else dr["w3"][e_]
            w2d = dr["sw2"] if shared else dr["w2"][e_]
            bi = 8 if shared else e_
            for grp, hcs in enumerate((range(0, 6), range(6, NC_H))):
                hcs = list(hcs)
                for hc in hcs:
                    hl = hc - hcs[0]
                    par = hc % 2
                    wb = w13bf[par]
                    if s == 0:
                        ld(w13st[:, 0:1024],
                           w1d[hc].rearrange("p n f -> p (n f)"),
                           "w13stA", W13ST_B[0])
                        ld(w13st[:, 1024:2048],
                           w3d[hc].rearrange("p n f -> p (n f)"),
                           "w13stB", W13ST_B[1])
                        em.op(G, lambda e, wb=wb: e.tensor_copy(wb, w13st),
                              reads=W13ST_B[0] + W13ST_B[1] + W13BF_B[par],
                              writes=W13BF_B[par])
                        em.dma("gpsimd_dma",
                               lambda e, wb=wb, bi=bi, hc=hc: e.dma_start(
                                   out=wc["w13"][bi, hc], in_=wb),
                               slot("wcs_w13"), reads=W13BF_B[par],
                               writes=[B(f"wc_w13_{bi}_{hc}"),
                                       B("wcc_w13")])
                    else:
                        em.dma("sync",
                               lambda e, wb=wb, bi=bi, hc=hc: e.dma_start(
                                   out=wb, in_=wc["w13"][bi, hc]),
                               slot(f"wcl_w13_{par}"),
                               reads=[B(f"wc_w13_{bi}_{hc}")] + W13BF_B[par],
                               writes=W13BF_B[par])
                    bcol1 = b1_c[:, bi * NC_H + hc: bi * NC_H + hc + 1]
                    bcol3 = b3_c[:, bi * NC_H + hc: bi * NC_H + hc + 1]
                    for q in range(4):
                        pu1 = (PA, PC)[q % 2]
                        pu3 = (PB, PD)[q % 2]
                        b_u1 = ("PA", "PC")[q % 2]
                        b_u3 = ("PB", "PD")[q % 2]
                        for dc in range(NC_D):
                            em.op(Tn, lambda e, pu1=pu1, wb=wb, dc=dc, q=q:
                                  e.matmul(
                                    pu1, wb[:, dc * 128:(dc + 1) * 128],
                                    xv(dc, q * 512, 512),
                                    start=(dc == 0), stop=(dc == NC_D - 1)),
                                  reads=W13BF_B[par] + [B("xn_T")],
                                  writes=[B(b_u1)])
                            em.op(Tn, lambda e, pu3=pu3, wb=wb, dc=dc, q=q:
                                  e.matmul(
                                    pu3,
                                    wb[:, 1024 + dc * 128:
                                       1024 + (dc + 1) * 128],
                                    xv(dc, q * 512, 512),
                                    start=(dc == 0), stop=(dc == NC_D - 1)),
                                  reads=W13BF_B[par] + [B("xn_T")],
                                  writes=[B(b_u3)])
                        us = u1s[q % 2]
                        em.op(V, lambda e, pu1=pu1, us=us, bcol1=bcol1:
                              e.tensor_scalar(us, pu1, bcol1, None, ALU.add),
                              reads=[B(b_u1)] + U1S_B[q % 2],
                              sreads=[B("b1_c")], writes=U1S_B[q % 2])
                        em.op(V, lambda e, pu3=pu3, us=us, bcol3=bcol3:
                              e.scalar_tensor_tensor(us, pu3, bcol3, us,
                                                     ALU.add, ALU.mult),
                              reads=[B(b_u3)] + U1S_B[q % 2],
                              sreads=[B("b3_c")], writes=U1S_B[q % 2])
                        em.op(S, lambda e, us=us, hl=hl, q=q: e.activation(
                                h_moe[:, hl * T + q * 512:
                                      hl * T + (q + 1) * 512],
                                us, AF.Silu),
                              reads=U1S_B[q % 2]
                              + A3BL(hl * T + q * 512, 512),
                              writes=A3BL(hl * T + q * 512, 512))
                    w2slice = w2_bf[:, hc * 1024:(hc + 1) * 1024]
                    if s == 0:
                        ld(w2st, w2d[hc], "w2st", W2ST_B)
                        em.op(G, lambda e, hc=hc, w2slice=w2slice:
                              e.tensor_copy(w2slice, w2st),
                              reads=W2ST_B + A5BL(hc * 1024, 1024),
                              writes=A5BL(hc * 1024, 1024))
                        em.dma("gpsimd_dma",
                               lambda e, w2slice=w2slice, bi=bi, hc=hc:
                               e.dma_start(out=wc["w2c"][bi, hc],
                                           in_=w2slice),
                               slot("wcs_w2c"), reads=A5BL(hc * 1024, 1024),
                               writes=[B(f"wc_w2c_{bi}_{hc}"),
                                       B("wcc_w2c")])
                    else:
                        em.dma("sync",
                               lambda e, w2slice=w2slice, bi=bi, hc=hc:
                               e.dma_start(out=w2slice,
                                           in_=wc["w2c"][bi, hc]),
                               slot(f"wcl_w2c_{hc}"),
                               reads=[B(f"wc_w2c_{bi}_{hc}")]
                               + A5BL(hc * 1024, 1024),
                               writes=A5BL(hc * 1024, 1024))
                first_grp = (grp == 0)
                for i in range(NT):
                    for dh in range(2):
                        po = (PE2[:, 0:512], PE2[:, 512:1024])[dh]
                        pob = ("PE2lo", "PE2hi")[dh]
                        for ki, hc in enumerate(hcs):
                            hl = hc - hcs[0]
                            last = (ki == len(hcs) - 1) and \
                                not (shared and first_grp)
                            em.op(Tn, lambda e, po=po, hc=hc, hl=hl, i=i,
                                  dh=dh, ki=ki, last=last: e.matmul(
                                    po,
                                    h_moe[:, hl * T + i * 128:
                                          hl * T + (i + 1) * 128],
                                    w2_bf[:, hc * 1024 + dh * 512:
                                          hc * 1024 + (dh + 1) * 512],
                                    start=(ki == 0), stop=last),
                                  reads=A3BL(hl * T + i * 128, 128)
                                  + A5BL(hc * 1024 + dh * 512, 512),
                                  writes=[B(pob)])
                        if shared and first_grp:
                            em.op(Tn, lambda e, po=po, i=i, dh=dh: e.matmul(
                                    po, rwT_bf[0:8, i * 128:(i + 1) * 128],
                                    b2all[0:8, dh * 512:(dh + 1) * 512],
                                    start=False, stop=False),
                                  reads=RWT_B + [B("b2all")],
                                  writes=[B(pob)])
                            em.op(Tn, lambda e, po=po, dh=dh: e.matmul(
                                    po, ones1,
                                    sb2_bf[0:1, dh * 512:(dh + 1) * 512],
                                    start=False, stop=True),
                                  reads=[B("ones1"), B("sb2_bf")],
                                  writes=[B(pob)])
                        ybl = A2BL(i * 1024 + dh * 512, 512)
                        if shared and first_grp:
                            em.op(V, lambda e, po=po, i=i, dh=dh:
                                  e.tensor_copy(
                                    y_acc[:, i * 1024 + dh * 512:
                                          i * 1024 + (dh + 1) * 512], po),
                                  reads=[B(pob)] + ybl, writes=ybl)
                        elif shared:
                            em.op(V, lambda e, po=po, i=i, dh=dh:
                                  e.tensor_tensor(
                                    y_acc[:, i * 1024 + dh * 512:
                                          i * 1024 + (dh + 1) * 512],
                                    y_acc[:, i * 1024 + dh * 512:
                                          i * 1024 + (dh + 1) * 512],
                                    po, ALU.add),
                                  reads=[B(pob)] + ybl, writes=ybl)
                        else:
                            rwc = rw_tok[:, i * 8 + e_: i * 8 + e_ + 1]
                            ya = y_acc[:, i * 1024 + dh * 512:
                                       i * 1024 + (dh + 1) * 512]
                            em.op(V, lambda e, po=po, ya=ya, rwc=rwc:
                                  e.scalar_tensor_tensor(ya, po, rwc, ya,
                                                         ALU.mult, ALU.add),
                                  reads=[B(pob)] + ybl,
                                  sreads=[B("rw_tok")], writes=ybl)
        for i in range(NT):
            par = i % 2
            gs = gst[par]
            em.op(S, lambda e, gs=gs, i=i: e.activation(
                    gs, y_acc[:, i * 1024:(i + 1) * 1024], AF.Silu),
                  reads=A2BL(i * 1024, 1024) + GST_B[par],
                  writes=GST_B[par])
            stdma(g_spill[s, i * 128:(i + 1) * 128, :], gs, f"gsp{par}",
                  [], writebuf=f"gd{par}", readbl=GST_B[par])
            if "dbg_g" in dbg_d:
                stdma(dbg_d["dbg_g"][s, i * 128:(i + 1) * 128, :], gs,
                      f"dbgg{par}", [], writebuf=f"gdd{par}",
                      readbl=GST_B[par])

    WQKST_B = A4BL(0, 1024)
    WQKBF_B = [A4BL(1024, 512), A4BL(1536, 512)]
    WVST_B = A4BL(2048, 1024)
    BND_B = A4BL(7168, 8)

    def seq_qkv(s):
        for i in range(NT):
            wbl = [B(f"A5_{4 * c + i // 4}") for c in range(8)]
            ld(ln1_full.rearrange("p (c t) -> p c t", t=T)
               [:, :, i * 128:(i + 1) * 128],
               ln1_spill[s, i], f"ln1ld{i // 4}", wbl,
               reads=[B(f"ln1d{i % 2}")])
        for jq in range(16):
            wpar = jq % 2
            wb = wqkbf[wpar]
            wstream(s, "qk", jq, wqkst, WQKST_B, "wqkst",
                    dr["w_qk"][jq].rearrange("p n f -> p (n f)"),
                    wb, WQKBF_B[wpar], wc["qk"][jq], sfx=str(wpar))
            dstq = q_T if jq < 8 else k_T
            base = 0 if jq < 8 else 16384
            c = jq % 8
            for tt in range(NTT):
                pq = (PA, PB)[tt % 2]
                pqb = ("PA", "PB")[tt % 2]
                for dc in range(NC_D):
                    em.op(Tn, lambda e, pq=pq, wb=wb, dc=dc, tt=tt: e.matmul(
                            pq, wb[:, dc * 128:(dc + 1) * 128],
                            ln1_full[:, dc * T + tt * 512:
                                     dc * T + (tt + 1) * 512],
                            start=(dc == 0), stop=(dc == NC_D - 1)),
                          reads=WQKBF_B[wpar]
                          + A5BL(dc * 2048 + tt * 512, 512),
                          writes=[B(pqb)])
                dbl = A2BLb(base + c * 2048 + tt * 512, 512)
                em.op(V, lambda e, pq=pq, dstq=dstq, c=c, tt=tt, jq=jq:
                      e.tensor_scalar(
                        dstq[:, c * T + tt * 512:c * T + (tt + 1) * 512],
                        pq, bqk_c[:, jq:jq + 1], None, ALU.add),
                      reads=[B(pqb)] + dbl, sreads=[B("bqk_c")], writes=dbl)
        em.op(G, lambda e: e.memset(v_aug, 1.0), writes=VAUG_B)
        for dc in range(NC_D):
            wstream(s, "wv", dc, wvst, WVST_B, "wvst", dr["w_v"][dc],
                    wv_bf[:, dc * 1024:(dc + 1) * 1024],
                    A4BL(3072 + dc * 512, 512), wc["wv"][dc],
                    sfx=str(dc))
        for i in range(NT):
            for jh in range(2):
                pv = (PC, PD)[(i * 2 + jh) % 2]
                pvb = ("PC", "PD")[(i * 2 + jh) % 2]
                for dc in range(NC_D):
                    em.op(Tn, lambda e, pv=pv, dc=dc, i=i, jh=jh: e.matmul(
                            pv,
                            ln1_full[:, dc * T + i * 128:
                                     dc * T + (i + 1) * 128],
                            wv_bf[:, dc * 1024 + jh * 512:
                                  dc * 1024 + (jh + 1) * 512],
                            start=(dc == 0), stop=False),
                          reads=A5BL(dc * 2048 + (i // 4) * 512, 512)
                          + A4BL(3072 + dc * 512 + jh * 256, 256),
                          writes=[B(pvb)])
                em.op(Tn, lambda e, pv=pv, jh=jh: e.matmul(
                        pv, ones1, bv_bf[0:1, jh * 512:(jh + 1) * 512],
                        start=False, stop=True),
                      reads=[B("ones1"), B("bv_bf")], writes=[B(pvb)])
                vbl = A3BL(i * 1040 + jh * 520, 520)
                em.op(V, lambda e, pv=pv, i=i, jh=jh: e.tensor_copy(
                        v_aug.rearrange("p (i h x) -> p i h x", i=NT, h=NH)
                        [:, i, jh * 8:(jh + 1) * 8, 0:64],
                        pv.rearrange("p (h x) -> p h x", x=64)),
                      reads=[B(pvb)] + vbl, writes=vbl)
        em.op(V, lambda e: e.tensor_reduce(bnd[:, 0:1], q_T, axis=AX.X,
                                           op=ALU.max,
                                           apply_absolute_value=True),
              reads=QT_B + BND_B, writes=BND_B)
        em.op(V, lambda e: e.tensor_reduce(bnd[:, 1:2], k_T, axis=AX.X,
                                           op=ALU.max,
                                           apply_absolute_value=True),
              reads=KT_B + BND_B, writes=BND_B)
        em.op(Tn, lambda e: e.transpose(PF2[0:1, 0:128], bnd[:, 0:1], id_f32),
              reads=BND_B + [B("id_f32")], writes=[B("PF2lo")])
        em.op(Tn, lambda e: e.transpose(PF2[0:1, 512:640], bnd[:, 1:2],
                                        id_f32),
              reads=BND_B + [B("id_f32")], writes=[B("PF2hi")])
        em.op(V, lambda e: e.reduce_max(bnd[0:1, 2:3], PF2[0:1, 0:128],
                                        axis=AX.X),
              reads=[B("PF2lo")] + BND_B, writes=BND_B)
        em.op(V, lambda e: e.reduce_max(bnd[0:1, 3:4], PF2[0:1, 512:640],
                                        axis=AX.X),
              reads=[B("PF2hi")] + BND_B, writes=BND_B)
        em.op(V, lambda e: e.tensor_scalar(bnd[0:1, 4:5], bnd[0:1, 2:3],
                                           bnd[0:1, 3:4], -8.0,
                                           ALU.mult, ALU.mult),
              reads=BND_B, sreads=BND_B, writes=BND_B)
        em.op(V, lambda e: e.tensor_scalar(bnd[0:1, 4:5], bnd[0:1, 4:5],
                                           -80.0, None, ALU.max),
              reads=BND_B, sreads=BND_B, writes=BND_B)
        em.op(Tn, lambda e: e.matmul(PF2[:, 0:1], ones1f, bnd[0:1, 4:5],
                                     start=True, stop=True),
              reads=[B("ones1f")] + BND_B, writes=[B("PF2lo")])
        em.op(V, lambda e: e.tensor_copy(negb, PF2[:, 0:1]),
              reads=[B("PF2lo"), B("negb")], writes=[B("negb")])
        if "dbg_q" in dbg_d:
            stdma(dbg_d["dbg_q"][s], q_T, "dbgq", [], readbl=QT_B)
            stdma(dbg_d["dbg_k"][s], k_T, "dbgk", [], readbl=KT_B)
            stdma(dbg_d["dbg_v"][s], v_aug, "dbgv", [], readbl=VAUG_B)

    RECB_B = A4BL(4096, 512)
    SROW_B = A4BL(4608, 512)

    def seq_attn(s):
        pend = []

        def emit_norm(py, pyb, c, pb, tt):
            em.op(V, lambda e, py=py: e.tensor_copy(srow[0:1, 0:512],
                                                    py[64:65, :]),
                  reads=[B(pyb)] + SROW_B, writes=SROW_B)
            em.op(V, lambda e: e.reciprocal(srow[0:1, 0:512],
                                            srow[0:1, 0:512]),
                  reads=SROW_B, writes=SROW_B)
            em.op(Tn, lambda e: e.matmul(PE2[:, 0:512], ones1f,
                                         srow[0:1, 0:512],
                                         start=True, stop=True),
                  reads=[B("ones1f")] + SROW_B, writes=[B("PE2lo")])
            em.op(V, lambda e: e.tensor_copy(recipb, PE2[:, 0:512]),
                  reads=[B("PE2lo")] + RECB_B, writes=RECB_B)
            ybl = A5BL(c * 2048 + tt * 512, 512)
            em.op(V, lambda e, py=py, c=c, pb=pb, tt=tt: e.tensor_tensor(
                    y_T[pb:pb + 64, c * T + tt * 512:
                        c * T + (tt + 1) * 512],
                    py[0:64, :], recipb[0:64, :], ALU.mult),
                  reads=[B(pyb)] + RECB_B + ybl, writes=ybl)

        blk = 0
        ypairs = [((PC, "PC"), (PD, "PD")),
                  ((PF2[:, 0:512], "PF2lo"), (PF2[:, 512:1024], "PF2hi"))]
        for hp in range(NH // 2):
            c = hp
            for tt in range(NTT):
                nsb = (tt + 1) * 4
                pyp = ypairs[blk % 2]
                for sb in range(nsb):
                    for hh in range(2):
                        k2 = sb * 2 + hh
                        pb = hh * 64
                        pp = (PA, PB)[k2 % 2]
                        ppb = ("PA", "PB")[k2 % 2]
                        em.op(Tn, lambda e, pp=pp, c=c, pb=pb, sb=sb, tt=tt:
                              e.matmul(
                                pp,
                                k_T[pb:pb + 64, c * T + sb * 128:
                                    c * T + (sb + 1) * 128],
                                q_T[pb:pb + 64, c * T + tt * 512:
                                    c * T + (tt + 1) * 512],
                                start=True, stop=True),
                              reads=A2BLb(16384 + c * 2048 + sb * 128, 128)
                              + A2BLb(c * 2048 + tt * 512, 512),
                              writes=[B(ppb)])
                        sl = k2 % 8
                        eb = expT[:, sl * 512:(sl + 1) * 512]
                        ebl = A4BL(sl * 256, 256)
                        em.op(S, lambda e, pp=pp, eb=eb: e.activation(
                                eb, pp, AF.Exp, bias=negb[:, 0:1],
                                scale=0.125),
                              reads=[B(ppb)] + ebl, sreads=[B("negb")],
                              writes=ebl)
                        o_ = sb * 128 - tt * 512
                        if 0 <= o_ < 512:
                            em.op(V, lambda e, eb=eb, o_=o_: e.tensor_tensor(
                                    eb, eb, mask_m[:, 512 - o_:1024 - o_],
                                    ALU.mult),
                                  reads=ebl + [B("mask_m")], writes=ebl)
                    for hh in range(2):
                        k2 = sb * 2 + hh
                        sl = k2 % 8
                        h = hp * 2 + hh
                        py, pyb = pyp[hh]
                        em.op(Tn, lambda e, py=py, sb=sb, h=h, sl=sl,
                              nsb=nsb: e.matmul(
                                py[0:65, :],
                                v_aug[:, sb * 1040 + h * 65:
                                      sb * 1040 + h * 65 + 65],
                                expT[:, sl * 512:(sl + 1) * 512],
                                start=(sb == 0), stop=(sb == nsb - 1)),
                              reads=A3BL(sb * 1040 + h * 65, 65)
                              + A4BL(sl * 256, 256),
                              writes=[B(pyb)])
                while pend:
                    emit_norm(*pend.pop(0))
                for hh in range(2):
                    py, pyb = pyp[hh]
                    pend.append((py, pyb, c, hh * 64, tt))
                blk += 1
        while pend:
            emit_norm(*pend.pop())
        if "dbg_y" in dbg_d:
            stdma(dbg_d["dbg_y"][s], y_T, "dbgy", [], readbl=YT_B)

    WPST_B = A4BL(5248, 1024)
    WPBF_B = [A4BL(6272, 512), A4BL(6784, 512)]

    def seq_proj(s):
        for jq in range(8):
            wpar = jq % 2
            wb = wpbf[wpar]
            wstream(s, "wp", jq, wpst, WPST_B, "wpst",
                    dr["w_proj"][jq].rearrange("p n f -> p (n f)"),
                    wb, WPBF_B[wpar], wc["wp"][jq], sfx=str(wpar))
            for tt in range(NTT):
                pp = (PA, PB)[tt % 2]
                ppb = ("PA", "PB")[tt % 2]
                for dc in range(NC_D):
                    em.op(Tn, lambda e, pp=pp, wb=wb, dc=dc, tt=tt: e.matmul(
                            pp, wb[:, dc * 128:(dc + 1) * 128],
                            y_T[:, dc * T + tt * 512:dc * T + (tt + 1) * 512],
                            start=(dc == 0), stop=(dc == NC_D - 1)),
                          reads=WPBF_B[wpar]
                          + A5BL(dc * 2048 + tt * 512, 512),
                          writes=[B(ppb)])
                dst = xv(jq, tt * 512, 512)
                em.op(V, lambda e, pp=pp, dst=dst, jq=jq:
                      e.scalar_tensor_tensor(dst, pp, bproj_c[:, jq:jq + 1],
                                             dst, ALU.add, ALU.add),
                      reads=[B(ppb), B("xn_T")], sreads=[B("bproj_c")],
                      writes=[B("xn_T")])
        if "dbg_t" in dbg_d:
            stdma(dbg_d["dbg_t"][s], xn_T, "dbgt", ["xn_T"])

    WFST_B = A4BL(0, 1024)
    WFBF_B = [A4BL(1024, 512), A4BL(1536, 512)]
    WMST_B = A4BL(2048, 2048)
    WMBF_B = [A4BL(4096, 1024), A4BL(5120, 1024)]
    AB2S_B = A4BL(6144, 1024)
    SQROW_B = A4BL(6144, 512)
    SQB_B = A4BL(7168, 512)
    MROW_B = A4BL(7680, 512)
    VROW_B = A4BL(8192, 512)

    def seq_mlp(s):
        for TTh in range(2):
            t0 = TTh * 1024
            for half in range(2):
                hb = t0 + half * 512
                for dc in range(NC_D):
                    em.op(Tn, lambda e, dc=dc, hb=hb: e.matmul(
                            PC[0:1, 0:512], ones128, xv(dc, hb, 512),
                            start=(dc == 0), stop=(dc == NC_D - 1)),
                          reads=[B("ones128"), B("xn_T")], writes=[B("PC")])
                for dc in range(NC_D):
                    em.op(S, lambda e, dc=dc, hb=hb: e.activation(
                            sqb[:, 0:512], xv(dc, hb, 512), AF.Square),
                          reads=[B("xn_T")] + SQB_B, writes=SQB_B)
                    em.op(Tn, lambda e, dc=dc: e.matmul(
                            PD[0:1, 0:512], ones128, sqb[:, 0:512],
                            start=(dc == 0), stop=(dc == NC_D - 1)),
                          reads=[B("ones128")] + SQB_B, writes=[B("PD")])
                em.op(V, lambda e: e.tensor_scalar(
                        mrow[0:1, 0:512], PC[0:1, 0:512], 1.0 / D, None,
                        ALU.mult),
                      reads=[B("PC")] + MROW_B, writes=MROW_B)
                em.op(V, lambda e: e.tensor_scalar(
                        vrow[0:1, 0:512], PD[0:1, 0:512], 1.0 / D, None,
                        ALU.mult),
                      reads=[B("PD")] + VROW_B, writes=VROW_B)
                em.op(S, lambda e: e.activation(sqrow[0:1, 0:512],
                                                mrow[0:1, 0:512], AF.Square),
                      reads=MROW_B + SQROW_B, writes=SQROW_B)
                em.op(V, lambda e: e.tensor_tensor(
                        vrow[0:1, 0:512], vrow[0:1, 0:512],
                        sqrow[0:1, 0:512], ALU.subtract),
                      reads=VROW_B + SQROW_B, writes=VROW_B)
                em.op(S, lambda e: e.activation(vrow[0:1, 0:512],
                                                vrow[0:1, 0:512], AF.Sqrt,
                                                bias=cpool[0:1, 2:3],
                                                scale=1.0),
                      reads=VROW_B, sreads=[B("cpool")], writes=VROW_B)
                em.op(V, lambda e: e.reciprocal(vrow[0:1, 0:512],
                                                vrow[0:1, 0:512]),
                      reads=VROW_B, writes=VROW_B)
                em.op(V, lambda e: e.tensor_tensor(
                        mrow[0:1, 0:512], mrow[0:1, 0:512],
                        vrow[0:1, 0:512], ALU.mult),
                      reads=MROW_B + VROW_B, writes=MROW_B)
                em.op(Tn, lambda e: e.matmul(PE2[:, 0:512], ones1f,
                                             vrow[0:1, 0:512],
                                             start=True, stop=True),
                      reads=[B("ones1f")] + VROW_B, writes=[B("PE2lo")])
                em.op(V, lambda e: e.tensor_copy(ab2s[:, 0:512],
                                                 PE2[:, 0:512]),
                      reads=[B("PE2lo")] + AB2S_B, writes=AB2S_B)
                em.op(Tn, lambda e: e.matmul(PE2[:, 512:1024], ones1f,
                                             mrow[0:1, 0:512],
                                             start=True, stop=True),
                      reads=[B("ones1f")] + MROW_B, writes=[B("PE2hi")])
                em.op(V, lambda e: e.tensor_copy(ab2s[:, 512:1024],
                                                 PE2[:, 512:1024]),
                      reads=[B("PE2hi")] + AB2S_B, writes=AB2S_B)
                for dc in range(NC_D):
                    lbl = A3BL(dc * 1024 + half * 512, 512)
                    em.op(V, lambda e, dc=dc, hb=hb: e.tensor_tensor(
                            sqb[:, 0:512], xv(dc, hb, 512),
                            ab2s[:, 0:512], ALU.mult),
                          reads=[B("xn_T")] + AB2S_B + SQB_B, writes=SQB_B)
                    em.op(V, lambda e, dc=dc, half=half: e.tensor_tensor(
                            ln2_bf[:, dc * 1024 + half * 512:
                                   dc * 1024 + (half + 1) * 512],
                            sqb[:, 0:512], ab2s[:, 512:1024], ALU.subtract),
                          reads=SQB_B + AB2S_B + lbl, writes=lbl)
            for jq in range(NC_F):
                wpar = jq % 2
                wb = wfbf[wpar]
                wstream((s, TTh), "wf", jq, wfst, WFST_B, "wfst",
                        dr["w_fc"][jq].rearrange("p n f -> p (n f)"),
                        wb, WFBF_B[wpar], wc["wf"][jq], sfx=str(wpar))
                pf = (PE2, PF2)[jq % 2]
                pfb = ("PE2", "PF2")[jq % 2]
                for half in range(2):
                    for dc in range(NC_D):
                        em.op(Tn, lambda e, pf=pf, wb=wb, dc=dc, half=half:
                              e.matmul(
                                pf[:, half * 512:(half + 1) * 512],
                                wb[:, dc * 128:(dc + 1) * 128],
                                ln2_bf[:, dc * 1024 + half * 512:
                                       dc * 1024 + (half + 1) * 512],
                                start=(dc == 0), stop=(dc == NC_D - 1)),
                              reads=WFBF_B[wpar]
                              + A3BL(dc * 1024 + half * 512, 512),
                              writes=[B(pfb + ("lo", "hi")[half])])
                hbl = A2BL(jq * 512, 512)
                em.op(S, lambda e, pf=pf, jq=jq: e.activation(
                        h_mlp[:, jq * 1024:(jq + 1) * 1024], pf,
                        AF.Gelu_apprx_tanh, bias=bfc_c[:, jq:jq + 1],
                        scale=1.0),
                      reads=[B(pfb + "lo"), B(pfb + "hi")] + hbl,
                      sreads=[B("bfc_c")], writes=hbl)
            for jc in range(8):
                for wh in range(2):
                    wb2 = wmbf[wh]
                    wstream((s, TTh), "wm", (jc, wh), wmst, WMST_B, "wmst",
                            dr["w_mlp"][jc, wh].rearrange("p n f -> p (n f)"),
                            wb2, WMBF_B[wh], wc["wm"][jc, wh], sfx=str(wh))
                for t5 in range(2):
                    pm = (PA, PB)[t5 % 2]
                    pmb = ("PA", "PB")[t5 % 2]
                    for hq in range(NC_F):
                        wb2 = wmbf[hq // 16]
                        em.op(Tn, lambda e, pm=pm, wb2=wb2, hq=hq, t5=t5:
                              e.matmul(
                                pm,
                                wb2[:, (hq % 16) * 128:(hq % 16 + 1) * 128],
                                h_mlp[:, hq * 1024 + t5 * 512:
                                      hq * 1024 + (t5 + 1) * 512],
                                start=(hq == 0), stop=(hq == NC_F - 1)),
                              reads=WMBF_B[hq // 16]
                              + A2BL(hq * 512 + t5 * 256, 256),
                              writes=[B(pmb)])
                    dst = s_T[:, jc * T + t0 + t5 * 512:
                              jc * T + t0 + (t5 + 1) * 512]
                    tsrc = xv(jc, t0 + t5 * 512, 512)
                    sbl = A5BL(jc * 2048 + t0 + t5 * 512, 512)
                    em.op(V, lambda e, pm=pm, dst=dst, tsrc=tsrc, jc=jc:
                          e.scalar_tensor_tensor(dst, pm,
                                                 bmlp_c[:, jc:jc + 1], tsrc,
                                                 ALU.add, ALU.add),
                          reads=[B(pmb), B("xn_T")] + sbl,
                          sreads=[B("bmlp_c")], writes=sbl)
        if "dbg_s" in dbg_d:
            stdma(dbg_d["dbg_s"][s], s_T, "dbgs", [], readbl=YT_B)

    CX_B = [A4BL(0, 1024), A4BL(1024, 1024)]
    CG_B = [A4BL(2048, 512), A4BL(2560, 512)]
    SGF_B = A4BL(3072, 1024)
    OST_B = [A4BL(4096, 1024), A4BL(5120, 1024)]

    def seq_comb(s):
        for i in range(NT):
            par = i % 2
            cx = cxst[par]
            cg = cgst[par]
            co = ost[par]
            ld(cx, x_d[s, i * 128:(i + 1) * 128, :], f"cx{par}", CX_B[par])
            ld(cg, g_spill[s, i * 128:(i + 1) * 128, :], f"cg{par}",
               CG_B[par], reads=[B(f"gd{par}")])
            pfull = (PA_bf, PB_bf)[par]
            ptrb = ("PA", "PB")[par]
            for c in range(NC_D):
                em.op(Tn, lambda e, pfull=pfull, c=c, i=i: e.transpose(
                        pfull[:, c * 128:(c + 1) * 128],
                        s_T[:, c * T + i * 128:c * T + (i + 1) * 128], id_bf),
                      reads=A5BL(c * 2048 + i * 128, 128) + [B("id_bf")],
                      writes=[B(ptrb)])
            em.op(V, lambda e, pfull=pfull, cg=cg: e.tensor_tensor(
                    sgf, pfull[:, 0:1024], cg, ALU.mult),
                  reads=[B(ptrb)] + CG_B[par] + SGF_B, writes=SGF_B)
            em.op(V, lambda e, cx=cx, co=co: e.tensor_tensor(
                    co, sgf, cx, ALU.add),
                  reads=SGF_B + CX_B[par] + OST_B[par], writes=OST_B[par])
            stdma(out_d[s, i * 128:(i + 1) * 128, :], co, f"outs{par}",
                  [], is_output=True, writebuf=f"od{par}",
                  readbl=OST_B[par])

    if cfg.get("dummy_out"):
        for s_ in range(nseq):
            for i in range(NT):
                stdma(out_d[s_, i * 128:(i + 1) * 128, :], kst,
                      "outs0", [], is_output=True, readbl=A4BL(0, 1024))
    PH_FN = {"ln": seq_ln, "gate": seq_gate, "moe": seq_moe, "qkv": seq_qkv,
             "attn": seq_attn, "proj": seq_proj, "mlp": seq_mlp,
             "comb": seq_comb}
    for s in range(nseq):
        for p in run_ph:
            em.scope = f"{p}{s}"
            PH_FN[p](s)

    em.finish()
    return nc



def _prep_weights(inp):
    C = np.ascontiguousarray
    w = {}
    wa = inp["L0_attn_w_attn"]
    ba = inp["L0_attn_b_attn"]
    w["w_qk"] = C(wa[:2048].T.reshape(8, 128, 16, 128).transpose(2, 1, 0, 3))
    w["b_qk"] = C(ba[:2048].reshape(16, 128).T)
    w["w_v"] = C(wa[2048:].T.reshape(8, 128, 1024))
    w["b_v"] = C(ba[2048:].reshape(1, 1024))
    w["w_proj"] = C(inp["L0_attn_w_proj"].T.reshape(8, 128, 8, 128)
                    .transpose(2, 1, 0, 3))
    w["b_proj"] = C(inp["L0_attn_b_proj"].reshape(8, 128).T)
    w["w_fc"] = C(inp["L0_attn_w_fc"].T.reshape(8, 128, 32, 128)
                  .transpose(2, 1, 0, 3))
    w["b_fc"] = C(inp["L0_attn_b_fc"].reshape(32, 128).T)
    w["w_mlp"] = C(inp["L0_attn_w_mlp_proj"].T
                   .reshape(2, 16, 128, 8, 128).transpose(3, 0, 2, 1, 4))
    w["b_mlp"] = C(inp["L0_attn_b_mlp_proj"].reshape(8, 128).T)
    w["gate_w"] = C(inp["L0_ffn_gate_w"].T.reshape(8, 128, 8)
                    .transpose(1, 0, 2))
    w["gate_b"] = C(inp["L0_ffn_gate_b"].reshape(1, 8))
    w["w1"] = C(inp["L0_ffn_w1"].reshape(NE, 8, 128, NC_H, 128)
                .transpose(0, 3, 2, 1, 4))
    w["w3"] = C(inp["L0_ffn_w3"].reshape(NE, 8, 128, NC_H, 128)
                .transpose(0, 3, 2, 1, 4))
    w["w2"] = C(inp["L0_ffn_w2"].reshape(NE, NC_H, 128, 1024))
    w["b1"] = C(inp["L0_ffn_b1"].reshape(NE, NC_H, 128).transpose(0, 2, 1))
    w["b3"] = C(inp["L0_ffn_b3"].reshape(NE, NC_H, 128).transpose(0, 2, 1))
    w["b2"] = C(inp["L0_ffn_b2"])
    w["sw1"] = C(inp["L0_ffn_sw1"].reshape(8, 128, NC_H, 128)
                 .transpose(2, 1, 0, 3))
    w["sw3"] = C(inp["L0_ffn_sw3"].reshape(8, 128, NC_H, 128)
                 .transpose(2, 1, 0, 3))
    w["sw2"] = C(inp["L0_ffn_sw2"].reshape(NC_H, 128, 1024))
    w["sb1"] = C(inp["L0_ffn_sb1"].reshape(NC_H, 128).T)
    w["sb3"] = C(inp["L0_ffn_sb3"].reshape(NC_H, 128).T)
    w["sb2"] = C(inp["L0_ffn_sb2"].reshape(1, 1024))
    return {k: np.asarray(v, dtype=np.float32) for k, v in w.items()}


def kernel(**inputs):
    inp = {k: np.asarray(v) for k, v in inputs.items()}
    w = _prep_weights(inp)
    x = np.ascontiguousarray(inp["L0_attn_x"], dtype=np.float32)
    nc = build_nc()
    in_maps = []
    for c in range(NCORE):
        m = dict(w)
        m["x"] = np.ascontiguousarray(x[c * NSEQ:(c + 1) * NSEQ])
        in_maps.append(m)
    res = run_bass_kernel_spmd(nc, in_maps, core_ids=list(range(NCORE)))
    out = np.concatenate([np.asarray(r["out"], dtype=np.float32)
                          for r in res.results], axis=0)
    return out


# version 07
# speedup vs baseline: 1.1708x; 1.1708x over previous
import numpy as np
import concourse.bass as bass
import concourse.mybir as mybir
from concourse.bass_utils import run_bass_kernel_spmd

AF = mybir.ActivationFunctionType
ALU = mybir.AluOpType
AX = mybir.AxisListType
F32 = mybir.dt.float32
BF16 = mybir.dt.bfloat16


COMPUTE_ENGINES = ("tensor", "vector", "scalar", "gpsimd")
DMA_QUEUES = {"sync": "sync", "gpsimd_dma": "gpsimd", "scalar_dma": "scalar"}
ALL_STREAMS = tuple(COMPUTE_ENGINES) + tuple(DMA_QUEUES)


class Buf:
    __slots__ = ("name", "writer", "readers")

    def __init__(self, name):
        self.name = name
        self.writer = None
        self.readers = []


class DmaSlot:
    __slots__ = ("name", "handle", "count", "is_output")

    def __init__(self, name, is_output=False):
        self.name = name
        self.handle = None
        self.count = 0
        self.is_output = is_output


class Op:
    __slots__ = ("engine", "fn", "deps", "need_inc", "sem", "val", "g",
                 "scope")

    def __init__(self, engine, fn, g, scope):
        self.engine = engine
        self.fn = fn
        self.deps = []
        self.need_inc = False
        self.sem = None
        self.val = 0
        self.g = g
        self.scope = scope


class Em:
    def __init__(self, nc, chain=False):
        self.nc = nc
        self.chain = chain
        self.ops = {e: [] for e in ALL_STREAMS}
        self._chain_buf = Buf("_chain")
        self._g = 0
        self._slots = []
        self.scope = "init"

    def op(self, engine, fn, reads=(), writes=(), sreads=()):
        o = Op(engine, fn, self._g, self.scope)
        self._g += 1
        self._track(o, list(reads), list(writes), list(sreads))
        self.ops[engine].append(o)
        return o

    def dma(self, queue, fn, slot, reads=(), writes=()):
        o = Op(queue, fn, self._g, self.scope)
        self._g += 1
        o.sem = slot
        self._track(o, list(reads), list(writes), [])
        self.ops[queue].append(o)
        return o

    def make_slot(self, name, is_output=False):
        s = DmaSlot(name, is_output)
        self._slots.append(s)
        return s

    def _track(self, o, reads, writes, sreads):
        if self.chain:
            reads = reads + [self._chain_buf]
            writes = writes + [self._chain_buf]
        deps = []
        for b in reads:
            if b.writer is not None:
                deps.append((b.writer, False))
        for b in sreads:
            if b.writer is not None:
                deps.append((b.writer, True))
        for b in writes:
            if b.writer is not None:
                deps.append((b.writer, False))
            deps.extend((r, False) for r in b.readers)
        strictmap = {}
        for d, strict in deps:
            if d is o:
                continue
            strictmap[id(d)] = strictmap.get(id(d), False) or strict
        added = set()
        for d, _ in deps:
            if d is o or id(d) in added:
                continue
            added.add(id(d))
            strict = strictmap[id(d)]
            if not strict and d.engine == o.engine == "tensor":
                continue
            o.deps.append(d)
            d.need_inc = True
        for b in reads:
            b.readers.append(o)
        for b in sreads:
            b.readers.append(o)
        for b in writes:
            b.writer = o
            b.readers = []

    def finish(self):
        nc = self.nc
        clock = {e: nc.alloc_semaphore(f"clk_{e}") for e in COMPUTE_ENGINES}
        for s in self._slots:
            s.handle = nc.alloc_semaphore(f"dma_{s.name}")
        for e in COMPUTE_ENGINES:
            v = 0
            for o in self.ops[e]:
                if o.need_inc:
                    v += 1
                o.val = v
                o.sem = clock[e]
        for q in DMA_QUEUES:
            for o in self.ops[q]:
                o.sem.count += 16
                o.val = o.sem.count
        streams = {}
        for name, lst in self.ops.items():
            phys = DMA_QUEUES.get(name, name)
            streams.setdefault(phys, []).extend(lst)
        for phys in streams:
            streams[phys].sort(key=lambda o: o.g)
        out_slots = [s for s in self._slots if s.is_output and s.count > 0]
        watermark = {}

        def emit_stream(phys, eng, lst, final):
            cur_scope = None
            cur_sid = None
            for o in lst:
                if o.scope != cur_scope:
                    if cur_scope is not None:
                        nc.leave_named_scope(f"ph_{cur_scope}", cur_sid, False)
                    cur_sid, _ = nc.enter_named_scope(f"ph_{o.scope}", False)
                    cur_scope = o.scope
                for d in o.deps:
                    sem = d.sem.handle if isinstance(d.sem, DmaSlot) else d.sem
                    key = (phys, id(sem))
                    if watermark.get(key, 0) >= d.val:
                        continue
                    eng.wait_ge(sem, d.val)
                    watermark[key] = d.val
                inst = o.fn(eng)
                if isinstance(o.sem, DmaSlot):
                    inst.then_inc(o.sem.handle, 16)
                elif o.need_inc:
                    inst.then_inc(o.sem, 1)
            if cur_scope is not None:
                nc.leave_named_scope(f"ph_{cur_scope}", cur_sid, False)
            if final:
                for s in out_slots:
                    key = (phys, id(s.handle))
                    if watermark.get(key, 0) < s.count:
                        eng.wait_ge(s.handle, s.count)
                        watermark[key] = s.count

        with nc.Block("main") as block:
            for phys in ["sync", "tensor", "vector", "scalar", "gpsimd"]:
                lst = streams.get(phys, [])

                def body(eng, lst=lst, phys=phys):
                    emit_stream(phys, eng, lst, final=(phys == "sync"))

                getattr(block, phys)(body)


D = 1024
T = 2048
NSEQ = 4
NCORE = 8
NH = 16
HME = 1408
NE = 8
DFF = 4096
NC_D = D // 128
NC_H = HME // 128
NC_F = DFF // 128
NT = T // 128
NTT = T // 512


def build_nc(cfg=None):
    cfg = cfg or {}
    nseq = cfg.get("nseq", NSEQ)
    upto = cfg.get("upto", "comb")
    dbg = set(cfg.get("debug", ()))

    PHASES = ["ln", "gate", "moe", "qkv", "attn", "proj", "mlp", "comb"]
    run_ph = [] if upto == "none" else PHASES[:PHASES.index(upto) + 1]

    nc = bass.Bass("TRN2", target_bir_lowering=False, debug=False)
    em = Em(nc, chain=cfg.get("chain", False))
    skip = set(cfg.get("skip", ()))

    dr = {}

    def din(name, shape):
        dr[name] = nc.dram_tensor(name, shape, F32, kind="ExternalInput").ap()

    din("x", [nseq, T, D])
    din("w_qk", [16, 128, 8, 128])
    din("b_qk", [128, 16])
    din("w_v", [8, 128, 1024])
    din("b_v", [1, 1024])
    din("w_proj", [8, 128, 8, 128])
    din("b_proj", [128, 8])
    din("w_fc", [32, 128, 8, 128])
    din("b_fc", [128, 32])
    din("w_mlp", [8, 2, 128, 16, 128])
    din("b_mlp", [128, 8])
    din("gate_w", [128, 8, 8])
    din("gate_b", [1, 8])
    din("w1", [NE, NC_H, 128, 8, 128])
    din("w3", [NE, NC_H, 128, 8, 128])
    din("w2", [NE, NC_H, 128, 1024])
    din("b1", [NE, 128, NC_H])
    din("b3", [NE, 128, NC_H])
    din("b2", [NE, 1024])
    din("sw1", [NC_H, 128, 8, 128])
    din("sw3", [NC_H, 128, 8, 128])
    din("sw2", [NC_H, 128, 1024])
    din("sb1", [128, NC_H])
    din("sb3", [128, NC_H])
    din("sb2", [1, 1024])
    x_d = dr["x"]

    out_d = nc.dram_tensor("out", [nseq, T, D], F32, kind="ExternalOutput").ap()
    g_spill = nc.dram_tensor("g_spill", [nseq, T, D], BF16, kind="Internal").ap()
    wc = {}
    for nm, shape in [("w13", [9, NC_H, 128, 2048]), ("w2c", [9, NC_H, 128, 1024]),
                      ("qk", [16, 128, 1024]), ("wv", [8, 128, 1024]),
                      ("wp", [8, 128, 1024]), ("wf", [32, 128, 1024]),
                      ("wm", [8, 2, 128, 2048])]:
        wc[nm] = nc.dram_tensor(f"wc_{nm}", shape, BF16, kind="Internal").ap()
    ln1_spill = nc.dram_tensor("ln1_spill", [nseq, NT, 128, D], BF16,
                               kind="Internal").ap()
    dbg_d = {}
    for name, shape, dt_ in [
            ("dbg_xnT", [nseq, 128, NC_D * T], BF16),
            ("dbg_rw", [nseq, 128, NT * 8], F32),
            ("dbg_g", [nseq, T, D], BF16),
            ("dbg_q", [nseq, 128, NC_D * T], BF16),
            ("dbg_k", [nseq, 128, NC_D * T], BF16),
            ("dbg_v", [nseq, 128, NT * 1040], BF16),
            ("dbg_y", [nseq, 128, NC_D * T], BF16),
            ("dbg_t", [nseq, 128, NC_D * T], BF16),
            ("dbg_s", [NSEQ, 128, NC_D * T], BF16)]:
        if name in dbg:
            dbg_d[name] = nc.dram_tensor(name, shape, dt_,
                                         kind="ExternalOutput").ap()

    sbt = {}

    def asb(name, shape, dtype):
        sbt[name] = nc.alloc_sbuf_tensor(name, shape, dtype).ap()
        return sbt[name]

    xn_T = asb("xn_T", [128, NC_D * T], BF16)
    A2 = asb("A2", [128, 16384], F32)
    A3 = asb("A3", [128, 16640], BF16)
    A4 = asb("A4", [128, 8704], F32)
    A5 = asb("A5", [128, 16384], BF16)

    A2bf = A2.bitcast(BF16)
    y_acc = A2
    q_T = A2bf[:, 0:NC_D * T]
    k_T = A2bf[:, NC_D * T:2 * NC_D * T]
    h_mlp = A2bf

    h_moe = A3[:, 0:6 * T]
    v_aug = A3[:, 0:NT * 1040]
    ln2_bf = A3[:, 0:NC_D * 1024]

    y_T = A5
    s_T = A5
    w2_bf = A5[:, 0:NC_H * 1024]
    ln1_full = A5

    A4bf = A4.bitcast(BF16)

    def a4f(c0, n):
        return A4[:, c0:c0 + n]

    def a4b(c0f, n):
        return A4bf[:, 2 * c0f:2 * c0f + n]

    kst = a4f(0, 1024)
    xst = [a4f(0, 1024), a4f(1024, 1024)]
    xn_f = a4f(2048, 1024)
    xn_b = a4b(3072, 1024)
    ln1_b = a4b(3584, 1024)
    st6s = [a4f(4096, 16), a4f(4608, 16)]
    gsc = a4f(4200, 32)
    rwT_bf = a4b(4300, 2048)
    w13st = a4f(0, 2048)
    w13bf = [a4b(2048, 2048), a4b(3072, 2048)]
    w2st = a4f(5376, 1024)
    u1s = [a4f(6400, 512), a4f(6912, 512)]
    gst = [a4b(7424, 1024), a4b(7936, 1024)]
    wqkst = a4f(0, 1024)
    wqkbf = [a4b(1024, 1024), a4b(1536, 1024)]
    wvst = a4f(2048, 1024)
    wv_bf = a4b(3072, 8192)
    bnd = a4f(7168, 8)
    expT = a4b(0, 8192)
    recipb = a4f(4096, 512)
    srow = a4f(4608, 512)
    wpst = a4f(5248, 1024)
    wpbf = [a4b(6272, 1024), a4b(6784, 1024)]
    wfst = a4f(0, 1024)
    wfbf = [a4b(1024, 1024), a4b(1536, 1024)]
    wmst = a4f(2048, 2048)
    wmbf = [a4b(4096, 2048), a4b(5120, 2048)]
    ab2s = a4f(6144, 1024)
    sqrow = a4f(6144, 512)
    sqb = a4b(7168, 1024)
    mrow = a4f(7680, 512)
    vrow = a4f(8192, 512)
    cxst = [a4f(0, 1024), a4f(1024, 1024)]
    cgst = [a4b(2048, 1024), a4b(2560, 1024)]
    sgf = a4f(3072, 1024)
    ost = [a4f(4096, 1024), a4f(5120, 1024)]

    ones128 = asb("ones128", [128, 1], BF16)
    ones1 = asb("ones1", [1, 128], BF16)
    ones1f = asb("ones1f", [1, 128], F32)
    id_bf = asb("id_bf", [128, 128], BF16)
    id_f32 = asb("id_f32", [128, 128], F32)
    mask_m = asb("mask_m", [128, 1024], BF16)
    cpool = asb("cpool", [128, 8], F32)
    CONSTS = {0.0: 0, 1e-5: 1, 1e-10: 2, 1.0: 3}

    def ccol(v):
        i = CONSTS[v]
        return cpool[:, i:i + 1]

    bqk_c = asb("bqk_c", [128, 16], F32)
    bproj_c = asb("bproj_c", [128, 8], F32)
    bfc_c = asb("bfc_c", [128, 32], F32)
    bmlp_c = asb("bmlp_c", [128, 8], F32)
    b1_c = asb("b1_c", [128, NC_H * 9], F32)
    b3_c = asb("b3_c", [128, NC_H * 9], F32)
    gw_bf = asb("gw_bf", [128, 64], BF16)
    gb_b = asb("gb_b", [128, 8], F32)
    gbst = asb("gbst", [1, 8], F32)
    b2all = asb("b2all", [8, 1024], BF16)
    bv_bf = asb("bv_bf", [1, 1024], BF16)
    sb2_bf = asb("sb2_bf", [1, 1024], BF16)
    negb = asb("negb", [128, 1], F32)
    rw_tok = asb("rw_tok", [128, NT * 8], F32)

    pst = {}

    def aps(name, shape):
        pst[name] = nc.alloc_psum_tensor(name, shape, F32).ap()
        return pst[name]

    PA = aps("PA", [128, 512])
    PB = aps("PB", [128, 512])
    PC = aps("PC", [128, 512])
    PD = aps("PD", [128, 512])
    PE2 = aps("PE2", [128, 1024])
    PF2 = aps("PF2", [128, 1024])
    PA_bf = PA.bitcast(BF16)
    PB_bf = PB.bitcast(BF16)
    PC_bf = PC.bitcast(BF16)
    PD_bf = PD.bitcast(BF16)

    bufs = {}

    def B(name):
        if name not in bufs:
            bufs[name] = Buf(name)
        return bufs[name]

    def _bl(arena, c0, n, blk=512):
        k0 = c0 // blk
        k1 = (c0 + n - 1) // blk
        return [B(f"{arena}_{k}") for k in range(k0, k1 + 1)]

    def A4BL(c0, n):
        return _bl("A4", c0, n)

    def A4BLb(c0b, nb):
        return _bl("A4", c0b // 2, (nb + 1) // 2)

    def A2BL(c0, n):
        return _bl("A2", c0, n)

    def A2BLb(c0b, nb):
        return _bl("A2", c0b // 2, (nb + 1) // 2)

    def A3BL(c0b, nb):
        return _bl("A3", c0b, nb)

    def A5BL(c0b, nb):
        return _bl("A5", c0b, nb)

    QT_B = A2BLb(0, 16384)
    KT_B = A2BLb(16384, 16384)
    YACC_B = A2BL(0, 16384)
    HMLP_B = A2BLb(0, 32768)
    HMOE_B = A3BL(0, 22528)
    VAUG_B = A3BL(0, 16640)
    LN2_B = A3BL(0, 8192)
    YT_B = A5BL(0, 16384)
    W2BF_B = A5BL(0, 11264)
    WVBF_B = A5BL(0, 8192)
    LN1F_B = A4BL(0, 8192)
    EXPT_B = A4BL(0, 4096)

    SL = {}

    def slot(name, is_output=False):
        if name not in SL:
            SL[name] = em.make_slot(name, is_output)
        return SL[name]

    V, S, Tn, G = "vector", "scalar", "tensor", "gpsimd"

    def ld(dst, src, slotname, writebuf, reads=()):
        wl = writebuf if isinstance(writebuf, list) else [B(writebuf)]
        em.dma("sync", lambda e, d=dst, s=src: e.dma_start(out=d, in_=s),
               slot(slotname), reads=list(reads), writes=wl)

    def stdma(dst, src, slotname, readbufs, is_output=False, writebuf=None,
              readbl=()):
        em.dma("gpsimd_dma",
               lambda e, d=dst, s=src: e.dma_start(out=d, in_=s),
               slot(slotname, is_output),
               reads=[B(x) for x in readbufs] + list(readbl),
               writes=[B(writebuf)] if writebuf else [])

    def wstream(s_, cls, key, stage_ap, stage_bl, stage_slot, f32_src,
                bf_ap, bf_bl, cache_ap, ceng="gpsimd", sfx=""):
        ebuf = B(f"wc_{cls}_{key}")
        if isinstance(s_, tuple):
            s_ = 0 if all(v == 0 for v in s_) else 1
        if s_ == 0:
            ld(stage_ap, f32_src, stage_slot, stage_bl)
            em.op(ceng, lambda e: e.tensor_copy(bf_ap, stage_ap),
                  reads=stage_bl + bf_bl, writes=bf_bl)
            em.dma("gpsimd_dma",
                   lambda e: e.dma_start(out=cache_ap, in_=bf_ap),
                   slot(f"wcs_{cls}"), reads=bf_bl,
                   writes=[ebuf, B(f"wcc_{cls}")])
        else:
            em.dma("sync", lambda e: e.dma_start(out=bf_ap, in_=cache_ap),
                   slot(f"wcl_{cls}{sfx}"), reads=[ebuf] + bf_bl,
                   writes=bf_bl)

    if not cfg.get("no_consts"):
        for v, i in CONSTS.items():
            em.op(G, lambda e, v=v, i=i: e.memset(cpool[:, i:i + 1], v),
                  writes=[B("cpool")])
        for t_, bn_, val in ((ones128, "ones128", 1.0), (ones1, "ones1", 1.0),
                             (ones1f, "ones1f", 1.0)):
            em.op(G, lambda e, t_=t_, val=val: e.memset(t_, val),
                  writes=[B(bn_)])
        for t_, bn_ in ((id_bf, "id_bf"), (id_f32, "id_f32")):
            em.op(G, lambda e, t_=t_: e.memset(t_, 0.0), writes=[B(bn_)])
            em.op(G, lambda e, t_=t_: e.affine_select(
                out=t_, in_=t_, compare_op=ALU.not_equal, fill=1.0, base=0,
                pattern=[[-1, 128]], channel_multiplier=1),
                reads=[B(bn_)], writes=[B(bn_)])
        em.op(G, lambda e: e.memset(mask_m, 1.0), writes=[B("mask_m")])
        em.op(G, lambda e: e.affine_select(
            out=mask_m, in_=mask_m, compare_op=ALU.is_ge, fill=0.0, base=-512,
            pattern=[[1, 1024]], channel_multiplier=-1),
            reads=[B("mask_m")], writes=[B("mask_m")])

        def kld(dst, src, bl):
            em.dma("sync", lambda e, d=dst, s=src: e.dma_start(out=d, in_=s),
                   slot("kconst"), reads=[], writes=bl + [B("kchain")])

        kld(bqk_c, dr["b_qk"], [B("bqk_c")])
        kld(bproj_c, dr["b_proj"], [B("bproj_c")])
        kld(bfc_c, dr["b_fc"], [B("bfc_c")])
        kld(bmlp_c, dr["b_mlp"], [B("bmlp_c")])
        for e_ in range(NE):
            kld(b1_c[:, e_ * NC_H:(e_ + 1) * NC_H], dr["b1"][e_], [B("b1_c")])
            kld(b3_c[:, e_ * NC_H:(e_ + 1) * NC_H], dr["b3"][e_], [B("b3_c")])
        kld(b1_c[:, 8 * NC_H:9 * NC_H], dr["sb1"], [B("b1_c")])
        kld(b3_c[:, 8 * NC_H:9 * NC_H], dr["sb3"], [B("b3_c")])
        KST_B = A4BL(0, 1024)
        kld(kst[:, 0:64], dr["gate_w"].rearrange("p a b -> p (a b)"), KST_B)
        em.op(V, lambda e: e.tensor_copy(gw_bf, kst[:, 0:64]),
              reads=KST_B + [B("gw_bf")], writes=[B("gw_bf")])
        kld(kst[0:8, :], dr["b2"], KST_B)
        em.op(V, lambda e: e.tensor_copy(b2all, kst[0:8, :]),
              reads=KST_B + [B("b2all")], writes=[B("b2all")])
        kld(kst[0:1, :], dr["b_v"], KST_B)
        em.op(V, lambda e: e.tensor_copy(bv_bf, kst[0:1, :]),
              reads=KST_B + [B("bv_bf")], writes=[B("bv_bf")])
        kld(kst[0:1, :], dr["sb2"], KST_B)
        em.op(V, lambda e: e.tensor_copy(sb2_bf, kst[0:1, :]),
              reads=KST_B + [B("sb2_bf")], writes=[B("sb2_bf")])
        kld(gbst, dr["gate_b"], [B("gbst")])
        em.op(Tn, lambda e: e.matmul(PA[:, 0:8], ones1f, gbst,
                                     start=True, stop=True),
              reads=[B("ones1f"), B("gbst")], writes=[B("PA")])
        em.op(V, lambda e: e.tensor_copy(gb_b, PA[:, 0:8]),
              reads=[B("PA")], writes=[B("gb_b")])

    def xv(c, lo, n):
        return xn_T[:, c * T + lo: c * T + lo + n]

    XST_B = [A4BL(0, 1024), A4BL(1024, 1024)]
    XNF_B = A4BL(2048, 1024)
    XNB_B = A4BL(3072, 512)
    LN1B_B = A4BL(3584, 512)
    ST6_BS = [A4BL(4096, 16), A4BL(4608, 16)]

    def seq_ln(s):
        for i in range(NT):
            par = i % 2
            xs = xst[par]
            st6 = st6s[par]
            ST6_B = ST6_BS[par]
            ld(xs, x_d[s, i * 128:(i + 1) * 128, :], f"xld{par}", XST_B[par])
            for (src, srcb, eps, dstb, dst) in (
                    (xs, XST_B[par], 1e-5, XNF_B, xn_f),
                    (xn_f, XNF_B, 1e-10, LN1B_B, ln1_b)):
                em.op(V, lambda e, src=src, st6=st6: e.bn_stats(
                        st6[:, 0:6], src[:, 0:512]),
                      reads=srcb + ST6_B, writes=ST6_B)
                em.op(V, lambda e, src=src, st6=st6: e.bn_stats(
                        st6[:, 6:12], src[:, 512:1024]),
                      reads=srcb + ST6_B, writes=ST6_B)
                em.op(V, lambda e, st6=st6: e.bn_aggr(st6[:, 12:14],
                                                      st6[:, 0:12]),
                      reads=ST6_B, writes=ST6_B)
                em.op(S, lambda e, eps=eps, st6=st6: e.activation(
                        st6[:, 14:15], st6[:, 13:14], AF.Sqrt,
                        bias=ccol(eps), scale=1.0),
                      reads=ST6_B, sreads=[B("cpool")], writes=ST6_B)
                em.op(V, lambda e, st6=st6: e.reciprocal(st6[:, 14:15],
                                                         st6[:, 14:15]),
                      reads=ST6_B, writes=ST6_B)
                em.op(V, lambda e, st6=st6: e.tensor_scalar(
                        st6[:, 15:16], st6[:, 12:13], st6[:, 14:15], -1.0,
                        ALU.mult, ALU.mult),
                      reads=ST6_B, sreads=ST6_B, writes=ST6_B)
                em.op(V, lambda e, src=src, dst=dst, st6=st6: e.tensor_scalar(
                        dst, src, st6[:, 14:15], st6[:, 15:16],
                        ALU.mult, ALU.add),
                      reads=srcb, sreads=ST6_B, writes=dstb)
            em.op(V, lambda e: e.tensor_copy(xn_b, xn_f),
                  reads=XNF_B + XNB_B, writes=XNB_B)
            for half in range(2):
                px = (PA_bf, PB_bf)[half]
                pxb = ("PA", "PB")[half]
                pl = (PC_bf, PD_bf)[half]
                plb = ("PC", "PD")[half]
                for c4 in range(4):
                    c = half * 4 + c4
                    em.op(Tn, lambda e, px=px, c4=c4, c=c: e.transpose(
                            px[:, c4 * 128:(c4 + 1) * 128],
                            xn_b[:, c * 128:(c + 1) * 128], id_bf),
                          reads=XNB_B + [B("id_bf")], writes=[B(pxb)])
                    em.op(Tn, lambda e, pl=pl, c4=c4, c=c: e.transpose(
                            pl[:, c4 * 128:(c4 + 1) * 128],
                            ln1_b[:, c * 128:(c + 1) * 128], id_bf),
                          reads=LN1B_B + [B("id_bf")], writes=[B(plb)])
                em.op(V, lambda e, px=px, half=half, i=i: e.tensor_copy(
                        xn_T.rearrange("p (c t) -> p c t", t=T)
                        [:, half * 4:(half + 1) * 4, i * 128:(i + 1) * 128],
                        px[:, 0:512].rearrange("p (c t) -> p c t", t=128)),
                      reads=[B(pxb), B("xn_T")], writes=[B("xn_T")])
                em.op(S, lambda e, pl=pl, half=half: e.copy(
                        ln1_b[:, half * 512:(half + 1) * 512], pl[:, 0:512]),
                      reads=[B(plb)] + LN1B_B, writes=LN1B_B)
            if "spill" not in skip:
                stdma(ln1_spill[s, i], ln1_b, f"ln1sp{par}",
                      [], is_output=False, writebuf=f"ln1d{par}",
                      readbl=LN1B_B)
        if "dbg_xnT" in dbg_d:
            stdma(dbg_d["dbg_xnT"][s], xn_T, "dbgx", ["xn_T"])

    GSC_B = A4BL(4200, 32)
    RWT_B = A4BL(4300, 1024)

    def seq_gate(s):
        for i in range(NT):
            pg = PA[:, 0:8]
            for dc in range(NC_D):
                em.op(Tn, lambda e, dc=dc, i=i: e.matmul(
                        pg, xv(dc, i * 128, 128),
                        gw_bf[:, dc * 8:(dc + 1) * 8],
                        start=(dc == 0), stop=(dc == NC_D - 1)),
                      reads=[B("xn_T"), B("gw_bf")], writes=[B("PA")])
            em.op(V, lambda e: e.reduce_max(gsc[:, 8:9], pg, axis=AX.X,
                                            negate=True),
                  reads=[B("PA")] + GSC_B, writes=GSC_B)
            em.op(S, lambda e: e.activation(gsc[:, 0:8], pg, AF.Exp,
                                            bias=gsc[:, 8:9], scale=1.0,
                                            accum_out=gsc[:, 9:10]),
                  reads=[B("PA")] + GSC_B, sreads=GSC_B, writes=GSC_B)
            em.op(V, lambda e: e.reciprocal(gsc[:, 9:10], gsc[:, 9:10]),
                  reads=GSC_B, writes=GSC_B)
            em.op(V, lambda e: e.tensor_scalar(gsc[:, 0:8], gsc[:, 0:8],
                                               gsc[:, 9:10], None, ALU.mult),
                  reads=GSC_B, sreads=GSC_B, writes=GSC_B)
            em.op(V, lambda e: e.tensor_tensor(gsc[:, 16:24], gsc[:, 0:8],
                                               gb_b, ALU.add),
                  reads=GSC_B + [B("gb_b")], writes=GSC_B)
            em.op(V, lambda e: e.reduce_max(gsc[:, 10:11], gsc[:, 16:24],
                                            axis=AX.X),
                  reads=GSC_B, writes=GSC_B)
            em.op(V, lambda e: e.tensor_scalar(gsc[:, 24:32], gsc[:, 16:24],
                                               gsc[:, 10:11], None, ALU.is_ge),
                  reads=GSC_B, sreads=GSC_B, writes=GSC_B)
            em.op(V, lambda e: e.scalar_tensor_tensor(
                    gsc[:, 24:32], gsc[:, 24:32], -1e30, gsc[:, 16:24],
                    ALU.mult, ALU.add),
                  reads=GSC_B, writes=GSC_B)
            em.op(V, lambda e: e.reduce_max(gsc[:, 11:12], gsc[:, 24:32],
                                            axis=AX.X),
                  reads=GSC_B, writes=GSC_B)
            em.op(V, lambda e: e.tensor_scalar(gsc[:, 24:32], gsc[:, 16:24],
                                               gsc[:, 11:12], None, ALU.is_ge),
                  reads=GSC_B, sreads=GSC_B, writes=GSC_B)
            em.op(V, lambda e, i=i: e.tensor_tensor(
                    rw_tok[:, i * 8:(i + 1) * 8], gsc[:, 0:8],
                    gsc[:, 24:32], ALU.mult),
                  reads=GSC_B + [B("rw_tok")], writes=[B("rw_tok")])
            em.op(Tn, lambda e, i=i: e.transpose(
                    PB[0:8, 0:128], rw_tok[:, i * 8:(i + 1) * 8], id_f32),
                  reads=[B("rw_tok"), B("id_f32")], writes=[B("PB")])
            em.op(V, lambda e, i=i: e.tensor_copy(
                    rwT_bf[0:8, i * 128:(i + 1) * 128], PB[0:8, 0:128]),
                  reads=[B("PB")] + RWT_B, writes=RWT_B)
        if "dbg_rw" in dbg_d:
            stdma(dbg_d["dbg_rw"][s], rw_tok, "dbgrw", ["rw_tok"])

    W13ST_B = [A4BL(0, 1024), A4BL(1024, 1024)]
    W13BF_B = [A4BL(2048, 1024), A4BL(3072, 1024)]
    W2ST_B = A4BL(5376, 1024)
    U1S_B = [A4BL(6400, 512), A4BL(6912, 512)]
    GST_B = [A4BL(7424, 512), A4BL(7936, 512)]

    def seq_moe(s):
        for e_ in [8] + list(range(NE)):
            shared = (e_ == 8)
            w1d = dr["sw1"] if shared else dr["w1"][e_]
            w3d = dr["sw3"] if shared else dr["w3"][e_]
            w2d = dr["sw2"] if shared else dr["w2"][e_]
            bi = 8 if shared else e_
            for grp, hcs in enumerate((range(0, 6), range(6, NC_H))):
                hcs = list(hcs)
                for hc in hcs:
                    hl = hc - hcs[0]
                    par = hc % 2
                    wb = w13bf[par]
                    if s == 0:
                        ld(w13st[:, 0:1024],
                           w1d[hc].rearrange("p n f -> p (n f)"),
                           "w13stA", W13ST_B[0])
                        ld(w13st[:, 1024:2048],
                           w3d[hc].rearrange("p n f -> p (n f)"),
                           "w13stB", W13ST_B[1])
                        em.op(G, lambda e, wb=wb: e.tensor_copy(wb, w13st),
                              reads=W13ST_B[0] + W13ST_B[1] + W13BF_B[par],
                              writes=W13BF_B[par])
                        em.dma("gpsimd_dma",
                               lambda e, wb=wb, bi=bi, hc=hc: e.dma_start(
                                   out=wc["w13"][bi, hc], in_=wb),
                               slot("wcs_w13"), reads=W13BF_B[par],
                               writes=[B(f"wc_w13_{bi}_{hc}"),
                                       B("wcc_w13")])
                    else:
                        em.dma("sync",
                               lambda e, wb=wb, bi=bi, hc=hc: e.dma_start(
                                   out=wb, in_=wc["w13"][bi, hc]),
                               slot(f"wcl_w13_{par}"),
                               reads=[B(f"wc_w13_{bi}_{hc}")] + W13BF_B[par],
                               writes=W13BF_B[par])
                    bcol1 = b1_c[:, bi * NC_H + hc: bi * NC_H + hc + 1]
                    bcol3 = b3_c[:, bi * NC_H + hc: bi * NC_H + hc + 1]
                    for q in range(4):
                        pu1 = (PA, PC)[q % 2]
                        pu3 = (PB, PD)[q % 2]
                        b_u1 = ("PA", "PC")[q % 2]
                        b_u3 = ("PB", "PD")[q % 2]
                        for dc in range(NC_D):
                            em.op(Tn, lambda e, pu1=pu1, wb=wb, dc=dc, q=q:
                                  e.matmul(
                                    pu1, wb[:, dc * 128:(dc + 1) * 128],
                                    xv(dc, q * 512, 512),
                                    start=(dc == 0), stop=(dc == NC_D - 1)),
                                  reads=W13BF_B[par] + [B("xn_T")],
                                  writes=[B(b_u1)])
                            em.op(Tn, lambda e, pu3=pu3, wb=wb, dc=dc, q=q:
                                  e.matmul(
                                    pu3,
                                    wb[:, 1024 + dc * 128:
                                       1024 + (dc + 1) * 128],
                                    xv(dc, q * 512, 512),
                                    start=(dc == 0), stop=(dc == NC_D - 1)),
                                  reads=W13BF_B[par] + [B("xn_T")],
                                  writes=[B(b_u3)])
                        us = u1s[q % 2]
                        em.op(V, lambda e, pu1=pu1, us=us, bcol1=bcol1:
                              e.tensor_scalar(us, pu1, bcol1, None, ALU.add),
                              reads=[B(b_u1)] + U1S_B[q % 2],
                              sreads=[B("b1_c")], writes=U1S_B[q % 2])
                        em.op(V, lambda e, pu3=pu3, us=us, bcol3=bcol3:
                              e.scalar_tensor_tensor(us, pu3, bcol3, us,
                                                     ALU.add, ALU.mult),
                              reads=[B(b_u3)] + U1S_B[q % 2],
                              sreads=[B("b3_c")], writes=U1S_B[q % 2])
                        em.op(S, lambda e, us=us, hl=hl, q=q: e.activation(
                                h_moe[:, hl * T + q * 512:
                                      hl * T + (q + 1) * 512],
                                us, AF.Silu),
                              reads=U1S_B[q % 2]
                              + A3BL(hl * T + q * 512, 512),
                              writes=A3BL(hl * T + q * 512, 512))
                    w2slice = w2_bf[:, hc * 1024:(hc + 1) * 1024]
                    if s == 0:
                        ld(w2st, w2d[hc], "w2st", W2ST_B)
                        em.op(G, lambda e, hc=hc, w2slice=w2slice:
                              e.tensor_copy(w2slice, w2st),
                              reads=W2ST_B + A5BL(hc * 1024, 1024),
                              writes=A5BL(hc * 1024, 1024))
                        em.dma("gpsimd_dma",
                               lambda e, w2slice=w2slice, bi=bi, hc=hc:
                               e.dma_start(out=wc["w2c"][bi, hc],
                                           in_=w2slice),
                               slot("wcs_w2c"), reads=A5BL(hc * 1024, 1024),
                               writes=[B(f"wc_w2c_{bi}_{hc}"),
                                       B("wcc_w2c")])
                    else:
                        em.dma("sync",
                               lambda e, w2slice=w2slice, bi=bi, hc=hc:
                               e.dma_start(out=w2slice,
                                           in_=wc["w2c"][bi, hc]),
                               slot(f"wcl_w2c_{hc}"),
                               reads=[B(f"wc_w2c_{bi}_{hc}")]
                               + A5BL(hc * 1024, 1024),
                               writes=A5BL(hc * 1024, 1024))
                first_grp = (grp == 0)
                for i in range(NT):
                    for dh in range(2):
                        po = (PE2[:, 0:512], PE2[:, 512:1024])[dh]
                        pob = ("PE2lo", "PE2hi")[dh]
                        for ki, hc in enumerate(hcs):
                            hl = hc - hcs[0]
                            last = (ki == len(hcs) - 1) and \
                                not (shared and first_grp)
                            em.op(Tn, lambda e, po=po, hc=hc, hl=hl, i=i,
                                  dh=dh, ki=ki, last=last: e.matmul(
                                    po,
                                    h_moe[:, hl * T + i * 128:
                                          hl * T + (i + 1) * 128],
                                    w2_bf[:, hc * 1024 + dh * 512:
                                          hc * 1024 + (dh + 1) * 512],
                                    start=(ki == 0), stop=last),
                                  reads=A3BL(hl * T + i * 128, 128)
                                  + A5BL(hc * 1024 + dh * 512, 512),
                                  writes=[B(pob)])
                        if shared and first_grp:
                            em.op(Tn, lambda e, po=po, i=i, dh=dh: e.matmul(
                                    po, rwT_bf[0:8, i * 128:(i + 1) * 128],
                                    b2all[0:8, dh * 512:(dh + 1) * 512],
                                    start=False, stop=False),
                                  reads=RWT_B + [B("b2all")],
                                  writes=[B(pob)])
                            em.op(Tn, lambda e, po=po, dh=dh: e.matmul(
                                    po, ones1,
                                    sb2_bf[0:1, dh * 512:(dh + 1) * 512],
                                    start=False, stop=True),
                                  reads=[B("ones1"), B("sb2_bf")],
                                  writes=[B(pob)])
                        ybl = A2BL(i * 1024 + dh * 512, 512)
                        if shared and first_grp:
                            em.op(V, lambda e, po=po, i=i, dh=dh:
                                  e.tensor_copy(
                                    y_acc[:, i * 1024 + dh * 512:
                                          i * 1024 + (dh + 1) * 512], po),
                                  reads=[B(pob)] + ybl, writes=ybl)
                        elif shared:
                            em.op(V, lambda e, po=po, i=i, dh=dh:
                                  e.tensor_tensor(
                                    y_acc[:, i * 1024 + dh * 512:
                                          i * 1024 + (dh + 1) * 512],
                                    y_acc[:, i * 1024 + dh * 512:
                                          i * 1024 + (dh + 1) * 512],
                                    po, ALU.add),
                                  reads=[B(pob)] + ybl, writes=ybl)
                        else:
                            rwc = rw_tok[:, i * 8 + e_: i * 8 + e_ + 1]
                            ya = y_acc[:, i * 1024 + dh * 512:
                                       i * 1024 + (dh + 1) * 512]
                            em.op(V, lambda e, po=po, ya=ya, rwc=rwc:
                                  e.scalar_tensor_tensor(ya, po, rwc, ya,
                                                         ALU.mult, ALU.add),
                                  reads=[B(pob)] + ybl,
                                  sreads=[B("rw_tok")], writes=ybl)
        for i in range(NT):
            par = i % 2
            gs = gst[par]
            em.op(S, lambda e, gs=gs, i=i: e.activation(
                    gs, y_acc[:, i * 1024:(i + 1) * 1024], AF.Silu),
                  reads=A2BL(i * 1024, 1024) + GST_B[par],
                  writes=GST_B[par])
            stdma(g_spill[s, i * 128:(i + 1) * 128, :], gs, f"gsp{par}",
                  [], writebuf=f"gd{par}", readbl=GST_B[par])
            if "dbg_g" in dbg_d:
                stdma(dbg_d["dbg_g"][s, i * 128:(i + 1) * 128, :], gs,
                      f"dbgg{par}", [], writebuf=f"gdd{par}",
                      readbl=GST_B[par])

    WQKST_B = A4BL(0, 1024)
    WQKBF_B = [A4BL(1024, 512), A4BL(1536, 512)]
    WVST_B = A4BL(2048, 1024)
    BND_B = A4BL(7168, 8)

    def seq_qkv(s):
        for i in range(NT):
            wbl = [B(f"A5_{4 * c + i // 4}") for c in range(8)]
            ld(ln1_full.rearrange("p (c t) -> p c t", t=T)
               [:, :, i * 128:(i + 1) * 128],
               ln1_spill[s, i], f"ln1ld{i // 4}", wbl,
               reads=[B(f"ln1d{i % 2}")])
        for jq in range(16):
            wpar = jq % 2
            wb = wqkbf[wpar]
            wstream(s, "qk", jq, wqkst, WQKST_B, "wqkst",
                    dr["w_qk"][jq].rearrange("p n f -> p (n f)"),
                    wb, WQKBF_B[wpar], wc["qk"][jq], sfx=str(wpar))
            dstq = q_T if jq < 8 else k_T
            base = 0 if jq < 8 else 16384
            c = jq % 8
            for tt in range(NTT):
                pq = (PA, PB)[tt % 2]
                pqb = ("PA", "PB")[tt % 2]
                for dc in range(NC_D):
                    em.op(Tn, lambda e, pq=pq, wb=wb, dc=dc, tt=tt: e.matmul(
                            pq, wb[:, dc * 128:(dc + 1) * 128],
                            ln1_full[:, dc * T + tt * 512:
                                     dc * T + (tt + 1) * 512],
                            start=(dc == 0), stop=(dc == NC_D - 1)),
                          reads=WQKBF_B[wpar]
                          + A5BL(dc * 2048 + tt * 512, 512),
                          writes=[B(pqb)])
                dbl = A2BLb(base + c * 2048 + tt * 512, 512)
                em.op(V, lambda e, pq=pq, dstq=dstq, c=c, tt=tt, jq=jq:
                      e.tensor_scalar(
                        dstq[:, c * T + tt * 512:c * T + (tt + 1) * 512],
                        pq, bqk_c[:, jq:jq + 1], None, ALU.add),
                      reads=[B(pqb)] + dbl, sreads=[B("bqk_c")], writes=dbl)
        em.op(G, lambda e: e.memset(v_aug, 1.0), writes=VAUG_B)
        for dc in range(NC_D):
            wstream(s, "wv", dc, wvst, WVST_B, "wvst", dr["w_v"][dc],
                    wv_bf[:, dc * 1024:(dc + 1) * 1024],
                    A4BL(3072 + dc * 512, 512), wc["wv"][dc],
                    sfx=str(dc))
        for i in range(NT):
            for jh in range(2):
                pv = (PC, PD)[(i * 2 + jh) % 2]
                pvb = ("PC", "PD")[(i * 2 + jh) % 2]
                for dc in range(NC_D):
                    em.op(Tn, lambda e, pv=pv, dc=dc, i=i, jh=jh: e.matmul(
                            pv,
                            ln1_full[:, dc * T + i * 128:
                                     dc * T + (i + 1) * 128],
                            wv_bf[:, dc * 1024 + jh * 512:
                                  dc * 1024 + (jh + 1) * 512],
                            start=(dc == 0), stop=False),
                          reads=A5BL(dc * 2048 + (i // 4) * 512, 512)
                          + A4BL(3072 + dc * 512 + jh * 256, 256),
                          writes=[B(pvb)])
                em.op(Tn, lambda e, pv=pv, jh=jh: e.matmul(
                        pv, ones1, bv_bf[0:1, jh * 512:(jh + 1) * 512],
                        start=False, stop=True),
                      reads=[B("ones1"), B("bv_bf")], writes=[B(pvb)])
                vbl = A3BL(i * 1040 + jh * 520, 520)
                em.op(V, lambda e, pv=pv, i=i, jh=jh: e.tensor_copy(
                        v_aug.rearrange("p (i h x) -> p i h x", i=NT, h=NH)
                        [:, i, jh * 8:(jh + 1) * 8, 0:64],
                        pv.rearrange("p (h x) -> p h x", x=64)),
                      reads=[B(pvb)] + vbl, writes=vbl)
        em.op(V, lambda e: e.tensor_reduce(bnd[:, 0:1], q_T, axis=AX.X,
                                           op=ALU.max,
                                           apply_absolute_value=True),
              reads=QT_B + BND_B, writes=BND_B)
        em.op(V, lambda e: e.tensor_reduce(bnd[:, 1:2], k_T, axis=AX.X,
                                           op=ALU.max,
                                           apply_absolute_value=True),
              reads=KT_B + BND_B, writes=BND_B)
        em.op(Tn, lambda e: e.transpose(PF2[0:1, 0:128], bnd[:, 0:1], id_f32),
              reads=BND_B + [B("id_f32")], writes=[B("PF2lo")])
        em.op(Tn, lambda e: e.transpose(PF2[0:1, 512:640], bnd[:, 1:2],
                                        id_f32),
              reads=BND_B + [B("id_f32")], writes=[B("PF2hi")])
        em.op(V, lambda e: e.reduce_max(bnd[0:1, 2:3], PF2[0:1, 0:128],
                                        axis=AX.X),
              reads=[B("PF2lo")] + BND_B, writes=BND_B)
        em.op(V, lambda e: e.reduce_max(bnd[0:1, 3:4], PF2[0:1, 512:640],
                                        axis=AX.X),
              reads=[B("PF2hi")] + BND_B, writes=BND_B)
        em.op(V, lambda e: e.tensor_scalar(bnd[0:1, 4:5], bnd[0:1, 2:3],
                                           bnd[0:1, 3:4], -8.0,
                                           ALU.mult, ALU.mult),
              reads=BND_B, sreads=BND_B, writes=BND_B)
        em.op(V, lambda e: e.tensor_scalar(bnd[0:1, 4:5], bnd[0:1, 4:5],
                                           -80.0, None, ALU.max),
              reads=BND_B, sreads=BND_B, writes=BND_B)
        em.op(Tn, lambda e: e.matmul(PF2[:, 0:1], ones1f, bnd[0:1, 4:5],
                                     start=True, stop=True),
              reads=[B("ones1f")] + BND_B, writes=[B("PF2lo")])
        em.op(V, lambda e: e.tensor_copy(negb, PF2[:, 0:1]),
              reads=[B("PF2lo"), B("negb")], writes=[B("negb")])
        if "dbg_q" in dbg_d:
            stdma(dbg_d["dbg_q"][s], q_T, "dbgq", [], readbl=QT_B)
            stdma(dbg_d["dbg_k"][s], k_T, "dbgk", [], readbl=KT_B)
            stdma(dbg_d["dbg_v"][s], v_aug, "dbgv", [], readbl=VAUG_B)

    RECB_B = A4BL(4096, 512)
    SROW_B = A4BL(4608, 512)

    def seq_attn(s):
        pend = []

        def emit_norm(py, pyb, c, pb, tt):
            em.op(V, lambda e, py=py: e.tensor_copy(srow[0:1, 0:512],
                                                    py[64:65, :]),
                  reads=[B(pyb)] + SROW_B, writes=SROW_B)
            em.op(V, lambda e: e.reciprocal(srow[0:1, 0:512],
                                            srow[0:1, 0:512]),
                  reads=SROW_B, writes=SROW_B)
            em.op(Tn, lambda e: e.matmul(PE2[:, 0:512], ones1f,
                                         srow[0:1, 0:512],
                                         start=True, stop=True),
                  reads=[B("ones1f")] + SROW_B, writes=[B("PE2lo")])
            em.op(V, lambda e: e.tensor_copy(recipb, PE2[:, 0:512]),
                  reads=[B("PE2lo")] + RECB_B, writes=RECB_B)
            ybl = A5BL(c * 2048 + tt * 512, 512)
            em.op(V, lambda e, py=py, c=c, pb=pb, tt=tt: e.tensor_tensor(
                    y_T[pb:pb + 64, c * T + tt * 512:
                        c * T + (tt + 1) * 512],
                    py[0:64, :], recipb[0:64, :], ALU.mult),
                  reads=[B(pyb)] + RECB_B + ybl, writes=ybl)

        blk = 0
        for h in range(NH):
            c = h // 2
            pb = (h % 2) * 64
            for tt in range(NTT):
                nsb = (tt + 1) * 4
                for sb in range(nsb):
                    pp = (PA, PB)[sb % 2]
                    ppb = ("PA", "PB")[sb % 2]
                    em.op(Tn, lambda e, pp=pp, c=c, pb=pb, sb=sb, tt=tt:
                          e.matmul(
                            pp,
                            k_T[pb:pb + 64, c * T + sb * 128:
                                c * T + (sb + 1) * 128],
                            q_T[pb:pb + 64, c * T + tt * 512:
                                c * T + (tt + 1) * 512],
                            start=True, stop=True),
                          reads=A2BLb(16384 + c * 2048 + sb * 128, 128)
                          + A2BLb(c * 2048 + tt * 512, 512),
                          writes=[B(ppb)])
                    eb = expT[:, sb * 512:(sb + 1) * 512]
                    ebl = A4BL(sb * 256, 256)
                    em.op(S, lambda e, pp=pp, eb=eb: e.activation(
                            eb, pp, AF.Exp, bias=negb[:, 0:1], scale=0.125),
                          reads=[B(ppb)] + ebl, sreads=[B("negb")],
                          writes=ebl)
                    o_ = sb * 128 - tt * 512
                    if 0 <= o_ < 512:
                        em.op(V, lambda e, eb=eb, o_=o_: e.tensor_tensor(
                                eb, eb, mask_m[:, 512 - o_:1024 - o_],
                                ALU.mult),
                              reads=ebl + [B("mask_m")], writes=ebl)
                py = (PC, PD)[blk % 2]
                pyb = ("PC", "PD")[blk % 2]
                for sb in range(nsb):
                    em.op(Tn, lambda e, py=py, sb=sb, h=h, nsb=nsb: e.matmul(
                            py[0:65, :],
                            v_aug[:, sb * 1040 + h * 65:
                                  sb * 1040 + h * 65 + 65],
                            expT[:, sb * 512:(sb + 1) * 512],
                            start=(sb == 0), stop=(sb == nsb - 1)),
                          reads=A3BL(sb * 1040 + h * 65, 65)
                          + A4BL(sb * 256, 256),
                          writes=[B(pyb)])
                if pend:
                    emit_norm(*pend.pop())
                pend.append((py, pyb, c, pb, tt))
                blk += 1
        while pend:
            emit_norm(*pend.pop())
        if "dbg_y" in dbg_d:
            stdma(dbg_d["dbg_y"][s], y_T, "dbgy", [], readbl=YT_B)

    WPST_B = A4BL(5248, 1024)
    WPBF_B = [A4BL(6272, 512), A4BL(6784, 512)]

    def seq_proj(s):
        for jq in range(8):
            wpar = jq % 2
            wb = wpbf[wpar]
            wstream(s, "wp", jq, wpst, WPST_B, "wpst",
                    dr["w_proj"][jq].rearrange("p n f -> p (n f)"),
                    wb, WPBF_B[wpar], wc["wp"][jq], sfx=str(wpar))
            for tt in range(NTT):
                pp = (PA, PB)[tt % 2]
                ppb = ("PA", "PB")[tt % 2]
                for dc in range(NC_D):
                    em.op(Tn, lambda e, pp=pp, wb=wb, dc=dc, tt=tt: e.matmul(
                            pp, wb[:, dc * 128:(dc + 1) * 128],
                            y_T[:, dc * T + tt * 512:dc * T + (tt + 1) * 512],
                            start=(dc == 0), stop=(dc == NC_D - 1)),
                          reads=WPBF_B[wpar]
                          + A5BL(dc * 2048 + tt * 512, 512),
                          writes=[B(ppb)])
                dst = xv(jq, tt * 512, 512)
                em.op(V, lambda e, pp=pp, dst=dst, jq=jq:
                      e.scalar_tensor_tensor(dst, pp, bproj_c[:, jq:jq + 1],
                                             dst, ALU.add, ALU.add),
                      reads=[B(ppb), B("xn_T")], sreads=[B("bproj_c")],
                      writes=[B("xn_T")])
        if "dbg_t" in dbg_d:
            stdma(dbg_d["dbg_t"][s], xn_T, "dbgt", ["xn_T"])

    WFST_B = A4BL(0, 1024)
    WFBF_B = [A4BL(1024, 512), A4BL(1536, 512)]
    WMST_B = A4BL(2048, 2048)
    WMBF_B = [A4BL(4096, 1024), A4BL(5120, 1024)]
    AB2S_B = A4BL(6144, 1024)
    SQROW_B = A4BL(6144, 512)
    SQB_B = A4BL(7168, 512)
    MROW_B = A4BL(7680, 512)
    VROW_B = A4BL(8192, 512)

    def seq_mlp(s):
        for TTh in range(2):
            t0 = TTh * 1024
            for half in range(2):
                hb = t0 + half * 512
                for dc in range(NC_D):
                    em.op(Tn, lambda e, dc=dc, hb=hb: e.matmul(
                            PC[0:1, 0:512], ones128, xv(dc, hb, 512),
                            start=(dc == 0), stop=(dc == NC_D - 1)),
                          reads=[B("ones128"), B("xn_T")], writes=[B("PC")])
                for dc in range(NC_D):
                    em.op(S, lambda e, dc=dc, hb=hb: e.activation(
                            sqb[:, 0:512], xv(dc, hb, 512), AF.Square),
                          reads=[B("xn_T")] + SQB_B, writes=SQB_B)
                    em.op(Tn, lambda e, dc=dc: e.matmul(
                            PD[0:1, 0:512], ones128, sqb[:, 0:512],
                            start=(dc == 0), stop=(dc == NC_D - 1)),
                          reads=[B("ones128")] + SQB_B, writes=[B("PD")])
                em.op(V, lambda e: e.tensor_scalar(
                        mrow[0:1, 0:512], PC[0:1, 0:512], 1.0 / D, None,
                        ALU.mult),
                      reads=[B("PC")] + MROW_B, writes=MROW_B)
                em.op(V, lambda e: e.tensor_scalar(
                        vrow[0:1, 0:512], PD[0:1, 0:512], 1.0 / D, None,
                        ALU.mult),
                      reads=[B("PD")] + VROW_B, writes=VROW_B)
                em.op(S, lambda e: e.activation(sqrow[0:1, 0:512],
                                                mrow[0:1, 0:512], AF.Square),
                      reads=MROW_B + SQROW_B, writes=SQROW_B)
                em.op(V, lambda e: e.tensor_tensor(
                        vrow[0:1, 0:512], vrow[0:1, 0:512],
                        sqrow[0:1, 0:512], ALU.subtract),
                      reads=VROW_B + SQROW_B, writes=VROW_B)
                em.op(S, lambda e: e.activation(vrow[0:1, 0:512],
                                                vrow[0:1, 0:512], AF.Sqrt,
                                                bias=cpool[0:1, 2:3],
                                                scale=1.0),
                      reads=VROW_B, sreads=[B("cpool")], writes=VROW_B)
                em.op(V, lambda e: e.reciprocal(vrow[0:1, 0:512],
                                                vrow[0:1, 0:512]),
                      reads=VROW_B, writes=VROW_B)
                em.op(V, lambda e: e.tensor_tensor(
                        mrow[0:1, 0:512], mrow[0:1, 0:512],
                        vrow[0:1, 0:512], ALU.mult),
                      reads=MROW_B + VROW_B, writes=MROW_B)
                em.op(Tn, lambda e: e.matmul(PE2[:, 0:512], ones1f,
                                             vrow[0:1, 0:512],
                                             start=True, stop=True),
                      reads=[B("ones1f")] + VROW_B, writes=[B("PE2lo")])
                em.op(V, lambda e: e.tensor_copy(ab2s[:, 0:512],
                                                 PE2[:, 0:512]),
                      reads=[B("PE2lo")] + AB2S_B, writes=AB2S_B)
                em.op(Tn, lambda e: e.matmul(PE2[:, 512:1024], ones1f,
                                             mrow[0:1, 0:512],
                                             start=True, stop=True),
                      reads=[B("ones1f")] + MROW_B, writes=[B("PE2hi")])
                em.op(V, lambda e: e.tensor_copy(ab2s[:, 512:1024],
                                                 PE2[:, 512:1024]),
                      reads=[B("PE2hi")] + AB2S_B, writes=AB2S_B)
                for dc in range(NC_D):
                    lbl = A3BL(dc * 1024 + half * 512, 512)
                    em.op(V, lambda e, dc=dc, hb=hb: e.tensor_tensor(
                            sqb[:, 0:512], xv(dc, hb, 512),
                            ab2s[:, 0:512], ALU.mult),
                          reads=[B("xn_T")] + AB2S_B + SQB_B, writes=SQB_B)
                    em.op(V, lambda e, dc=dc, half=half: e.tensor_tensor(
                            ln2_bf[:, dc * 1024 + half * 512:
                                   dc * 1024 + (half + 1) * 512],
                            sqb[:, 0:512], ab2s[:, 512:1024], ALU.subtract),
                          reads=SQB_B + AB2S_B + lbl, writes=lbl)
            for jq in range(NC_F):
                wpar = jq % 2
                wb = wfbf[wpar]
                wstream((s, TTh), "wf", jq, wfst, WFST_B, "wfst",
                        dr["w_fc"][jq].rearrange("p n f -> p (n f)"),
                        wb, WFBF_B[wpar], wc["wf"][jq], sfx=str(wpar))
                pf = (PE2, PF2)[jq % 2]
                pfb = ("PE2", "PF2")[jq % 2]
                for half in range(2):
                    for dc in range(NC_D):
                        em.op(Tn, lambda e, pf=pf, wb=wb, dc=dc, half=half:
                              e.matmul(
                                pf[:, half * 512:(half + 1) * 512],
                                wb[:, dc * 128:(dc + 1) * 128],
                                ln2_bf[:, dc * 1024 + half * 512:
                                       dc * 1024 + (half + 1) * 512],
                                start=(dc == 0), stop=(dc == NC_D - 1)),
                              reads=WFBF_B[wpar]
                              + A3BL(dc * 1024 + half * 512, 512),
                              writes=[B(pfb + ("lo", "hi")[half])])
                hbl = A2BL(jq * 512, 512)
                em.op(S, lambda e, pf=pf, jq=jq: e.activation(
                        h_mlp[:, jq * 1024:(jq + 1) * 1024], pf,
                        AF.Gelu_apprx_tanh, bias=bfc_c[:, jq:jq + 1],
                        scale=1.0),
                      reads=[B(pfb + "lo"), B(pfb + "hi")] + hbl,
                      sreads=[B("bfc_c")], writes=hbl)
            for jc in range(8):
                for wh in range(2):
                    wb2 = wmbf[wh]
                    wstream((s, TTh), "wm", (jc, wh), wmst, WMST_B, "wmst",
                            dr["w_mlp"][jc, wh].rearrange("p n f -> p (n f)"),
                            wb2, WMBF_B[wh], wc["wm"][jc, wh], sfx=str(wh))
                for t5 in range(2):
                    pm = (PA, PB)[t5 % 2]
                    pmb = ("PA", "PB")[t5 % 2]
                    for hq in range(NC_F):
                        wb2 = wmbf[hq // 16]
                        em.op(Tn, lambda e, pm=pm, wb2=wb2, hq=hq, t5=t5:
                              e.matmul(
                                pm,
                                wb2[:, (hq % 16) * 128:(hq % 16 + 1) * 128],
                                h_mlp[:, hq * 1024 + t5 * 512:
                                      hq * 1024 + (t5 + 1) * 512],
                                start=(hq == 0), stop=(hq == NC_F - 1)),
                              reads=WMBF_B[hq // 16]
                              + A2BL(hq * 512 + t5 * 256, 256),
                              writes=[B(pmb)])
                    dst = s_T[:, jc * T + t0 + t5 * 512:
                              jc * T + t0 + (t5 + 1) * 512]
                    tsrc = xv(jc, t0 + t5 * 512, 512)
                    sbl = A5BL(jc * 2048 + t0 + t5 * 512, 512)
                    em.op(V, lambda e, pm=pm, dst=dst, tsrc=tsrc, jc=jc:
                          e.scalar_tensor_tensor(dst, pm,
                                                 bmlp_c[:, jc:jc + 1], tsrc,
                                                 ALU.add, ALU.add),
                          reads=[B(pmb), B("xn_T")] + sbl,
                          sreads=[B("bmlp_c")], writes=sbl)
        if "dbg_s" in dbg_d:
            stdma(dbg_d["dbg_s"][s], s_T, "dbgs", [], readbl=YT_B)

    CX_B = [A4BL(0, 1024), A4BL(1024, 1024)]
    CG_B = [A4BL(2048, 512), A4BL(2560, 512)]
    SGF_B = A4BL(3072, 1024)
    OST_B = [A4BL(4096, 1024), A4BL(5120, 1024)]

    def seq_comb(s):
        for i in range(NT):
            par = i % 2
            cx = cxst[par]
            cg = cgst[par]
            co = ost[par]
            ld(cx, x_d[s, i * 128:(i + 1) * 128, :], f"cx{par}", CX_B[par])
            ld(cg, g_spill[s, i * 128:(i + 1) * 128, :], f"cg{par}",
               CG_B[par], reads=[B(f"gd{par}")])
            pfull = (PA_bf, PB_bf)[par]
            ptrb = ("PA", "PB")[par]
            for c in range(NC_D):
                em.op(Tn, lambda e, pfull=pfull, c=c, i=i: e.transpose(
                        pfull[:, c * 128:(c + 1) * 128],
                        s_T[:, c * T + i * 128:c * T + (i + 1) * 128], id_bf),
                      reads=A5BL(c * 2048 + i * 128, 128) + [B("id_bf")],
                      writes=[B(ptrb)])
            em.op(V, lambda e, pfull=pfull, cg=cg: e.tensor_tensor(
                    sgf, pfull[:, 0:1024], cg, ALU.mult),
                  reads=[B(ptrb)] + CG_B[par] + SGF_B, writes=SGF_B)
            em.op(V, lambda e, cx=cx, co=co: e.tensor_tensor(
                    co, sgf, cx, ALU.add),
                  reads=SGF_B + CX_B[par] + OST_B[par], writes=OST_B[par])
            stdma(out_d[s, i * 128:(i + 1) * 128, :], co, f"outs{par}",
                  [], is_output=True, writebuf=f"od{par}",
                  readbl=OST_B[par])

    if cfg.get("dummy_out"):
        for s_ in range(nseq):
            for i in range(NT):
                stdma(out_d[s_, i * 128:(i + 1) * 128, :], kst,
                      "outs0", [], is_output=True, readbl=A4BL(0, 1024))
    PH_FN = {"ln": seq_ln, "gate": seq_gate, "moe": seq_moe, "qkv": seq_qkv,
             "attn": seq_attn, "proj": seq_proj, "mlp": seq_mlp,
             "comb": seq_comb}
    for s in range(nseq):
        for p in run_ph:
            em.scope = f"{p}{s}"
            PH_FN[p](s)

    em.finish()
    return nc



def _prep_weights(inp):
    C = np.ascontiguousarray
    w = {}
    wa = inp["L0_attn_w_attn"]
    ba = inp["L0_attn_b_attn"]
    w["w_qk"] = C(wa[:2048].T.reshape(8, 128, 16, 128).transpose(2, 1, 0, 3))
    w["b_qk"] = C(ba[:2048].reshape(16, 128).T)
    w["w_v"] = C(wa[2048:].T.reshape(8, 128, 1024))
    w["b_v"] = C(ba[2048:].reshape(1, 1024))
    w["w_proj"] = C(inp["L0_attn_w_proj"].T.reshape(8, 128, 8, 128)
                    .transpose(2, 1, 0, 3))
    w["b_proj"] = C(inp["L0_attn_b_proj"].reshape(8, 128).T)
    w["w_fc"] = C(inp["L0_attn_w_fc"].T.reshape(8, 128, 32, 128)
                  .transpose(2, 1, 0, 3))
    w["b_fc"] = C(inp["L0_attn_b_fc"].reshape(32, 128).T)
    w["w_mlp"] = C(inp["L0_attn_w_mlp_proj"].T
                   .reshape(2, 16, 128, 8, 128).transpose(3, 0, 2, 1, 4))
    w["b_mlp"] = C(inp["L0_attn_b_mlp_proj"].reshape(8, 128).T)
    w["gate_w"] = C(inp["L0_ffn_gate_w"].T.reshape(8, 128, 8)
                    .transpose(1, 0, 2))
    w["gate_b"] = C(inp["L0_ffn_gate_b"].reshape(1, 8))
    w["w1"] = C(inp["L0_ffn_w1"].reshape(NE, 8, 128, NC_H, 128)
                .transpose(0, 3, 2, 1, 4))
    w["w3"] = C(inp["L0_ffn_w3"].reshape(NE, 8, 128, NC_H, 128)
                .transpose(0, 3, 2, 1, 4))
    w["w2"] = C(inp["L0_ffn_w2"].reshape(NE, NC_H, 128, 1024))
    w["b1"] = C(inp["L0_ffn_b1"].reshape(NE, NC_H, 128).transpose(0, 2, 1))
    w["b3"] = C(inp["L0_ffn_b3"].reshape(NE, NC_H, 128).transpose(0, 2, 1))
    w["b2"] = C(inp["L0_ffn_b2"])
    w["sw1"] = C(inp["L0_ffn_sw1"].reshape(8, 128, NC_H, 128)
                 .transpose(2, 1, 0, 3))
    w["sw3"] = C(inp["L0_ffn_sw3"].reshape(8, 128, NC_H, 128)
                 .transpose(2, 1, 0, 3))
    w["sw2"] = C(inp["L0_ffn_sw2"].reshape(NC_H, 128, 1024))
    w["sb1"] = C(inp["L0_ffn_sb1"].reshape(NC_H, 128).T)
    w["sb3"] = C(inp["L0_ffn_sb3"].reshape(NC_H, 128).T)
    w["sb2"] = C(inp["L0_ffn_sb2"].reshape(1, 1024))
    return {k: np.asarray(v, dtype=np.float32) for k, v in w.items()}


def kernel(**inputs):
    inp = {k: np.asarray(v) for k, v in inputs.items()}
    w = _prep_weights(inp)
    x = np.ascontiguousarray(inp["L0_attn_x"], dtype=np.float32)
    nc = build_nc()
    in_maps = []
    for c in range(NCORE):
        m = dict(w)
        m["x"] = np.ascontiguousarray(x[c * NSEQ:(c + 1) * NSEQ])
        in_maps.append(m)
    res = run_bass_kernel_spmd(nc, in_maps, core_ids=list(range(NCORE)))
    out = np.concatenate([np.asarray(r["out"], dtype=np.float32)
                          for r in res.results], axis=0)
    return out
